# version 31
# baseline (speedup 1.0000x reference)
import math
from contextlib import ExitStack

import numpy as np
import ml_dtypes
import concourse.bass as bass
import concourse.mybir as mybir
from concourse.bass_utils import run_bass_kernel_spmd

F32 = mybir.dt.float32
BF16 = mybir.dt.bfloat16
FP8 = mybir.dt.float8e4
I32 = mybir.dt.int32
AF = mybir.ActivationFunctionType
OP = mybir.AluOpType
AX = mybir.AxisListType

E = 2048
H = 16
KVH = 4
D = 128
IFF = 5632
SQD = 1.0 / math.sqrt(D)
LAM_INIT = 0.2
OUT_GAIN = 0.8


class Sem:
    __slots__ = ("h", "n", "eng")

    def __init__(self, h, eng=None):
        self.h = h
        self.n = 0
        self.eng = eng


class Buf:
    __slots__ = ("w", "rs", "dsem")

    def __init__(self):
        self.w = None
        self.rs = []
        self.dsem = None


class Prog:
    ENGS = ("sync", "tensor", "vector", "scalar", "gpsimd")

    def __init__(self, nc, es):
        self.nc = nc
        self.es = es
        self.items = {e: [] for e in self.ENGS}
        self._hw = {e: {} for e in self.ENGS}
        self._nds = 0
        self.dsems = []
        self.csem = {e: Sem(es.enter_context(nc.semaphore("c_" + e)), e)
                     for e in ("tensor", "vector", "scalar", "gpsimd", "sync")}
        self._tr_last = None

    def buf_dsem(self, b):
        if b.dsem is None:
            self._nds += 1
            b.dsem = Sem(self.es.enter_context(
                self.nc.semaphore(f"d{self._nds}")), None)
            self.dsems.append(b.dsem)
        return b.dsem

    def _waits(self, eng, toks, never_skip=False):
        out = []
        hw = self._hw[eng]
        for t in toks:
            if t is None:
                continue
            sem, val = t
            if val <= 0:
                continue
            if hw.get(id(sem), 0) >= val:
                continue
            hw[id(sem)] = val
            out.append((sem, val))
        return out

    def op(self, eng, fn, rd=(), rdp=(), wr=(), extra=(), sem=None, inck=1):
        toks = []
        for b in rd:
            toks.append(b.w)
        for b in wr:
            toks.append(b.w)
            toks.extend(b.rs)
        ptoks = [b.w for b in rdp]
        ws = self._waits(eng, toks + list(extra)) + \
            self._waits(eng, ptoks, never_skip=True)
        s = sem if sem is not None else self.csem[eng]
        s.n += inck
        tok = (s, s.n)
        self.items[eng].append((ws, fn, s, inck))
        for b in rd:
            b.rs.append(tok)
        for b in rdp:
            b.rs.append(tok)
        for b in wr:
            b.w = tok
            b.rs = []
        return tok

    def dma(self, eng, dst, src, rd=(), wr=(), extra=(), **kw):
        sem = self.buf_dsem(wr[0])
        return self.op(eng, lambda e: e.dma_start(dst, src, **kw),
                       rd=rd, wr=wr, extra=extra, sem=sem, inck=16)

    def trans(self, dst, src, rd=(), wr=(), extra=()):
        sem = self.buf_dsem(wr[0])
        tok = self.op("sync", lambda e: e.dma_start_transpose(dst, src),
                      rd=rd, wr=wr,
                      extra=list(extra) + ([self._tr_last]
                                           if self._tr_last else []),
                      sem=sem, inck=16)
        self._tr_last = tok
        return tok

    def barrier(self, sems):
        snap = [(s, s.n) for s in sems if s.n > 0]
        for e in self.ENGS:
            hw = self._hw[e]
            ws = []
            for (s, v) in snap:
                if s.eng == e or hw.get(id(s), 0) >= v:
                    continue
                hw[id(s)] = v
                ws.append((s, v))
            if ws:
                self.items[e].append((ws, None, None, 0))

    def emit(self):
        nc = self.nc
        with nc.Block() as block:
            def mk(name):
                def body(eng):
                    for (ws, fn, sem, inck) in self.items[name]:
                        for (s, v) in ws:
                            eng.wait_ge(s.h, v)
                        if fn is None:
                            continue
                        ins = fn(eng)
                        if sem is not None:
                            ins.then_inc(sem.h, inck)
                return body
            for e in self.ENGS:
                getattr(block, e)(mk(e))

    def n_instr(self):
        return {e: (len(v), sum(len(w[0]) for w in v)) for e, v in self.items.items()}


def build_nc(S=2048, phases="ABCD", dbg=(), KCH=1024):
    NT = S // 128
    nc = bass.Bass("TRN2", target_bir_lowering=False)

    din = lambda name, shape, dt: nc.dram_tensor(name, list(shape), dt,
                                                 kind="ExternalInput")
    x_in = din("x", [S, E], F32)
    wq = din("wq", [E, 2 * H * D], F32)
    wk = din("wk", [E, 2 * KVH * D], F32)
    wv = din("wv", [E, KVH * D], F32)
    wo_t = din("wo", [E, E], F32)
    wg_t = din("wg", [IFF, E], F32)
    wu_t = din("wu", [IFF, E], F32)
    wd_t = din("wd", [E, IFF], F32)
    idx_t = din("idx", [IFF, 1], I32)
    lq1 = din("lq1", [H, D], F32)
    lq2 = din("lq2", [H, D], F32)
    lk1 = din("lk1", [H, D], F32)
    lk2 = din("lk2", [H, D], F32)
    rmsw = din("rmsw", [D, 1], F32)
    costab = din("costab", [S, 256], BF16)
    sintab = din("sintab", [S, 256], BF16)
    tri_t = din("tri", [128, 128], BF16)
    identq_t = din("identq", [128, 128], FP8)
    identf_t = din("identf", [128, 128], F32)

    qs = nc.dram_tensor("qs", [NT * 4096, 128], BF16)
    xm = nc.dram_tensor("xm", [S, E], F32)
    tsc = nc.dram_tensor("tsc", [S, IFF], BF16)
    ysc = nc.dram_tensor("ysc", [S, E], BF16)
    out_t = nc.dram_tensor("out", [S, E], F32, kind="ExternalOutput")

    dbg_t = {}
    def dbg_out(name, shape, dt):
        if name in dbg:
            dbg_t[name] = nc.dram_tensor("dbg_" + name, list(shape), dt,
                                         kind="ExternalOutput")
        return dbg_t.get(name)

    dbg_out("lnxT", [128, 16 * S], BF16)
    dbg_out("KT", [128, 8 * S], BF16)
    dbg_out("V", [128, KVH * NT * 132], BF16)
    dbg_out("qs", [NT * 4096, 128], BF16)
    dbg_out("xmid", [S, E], F32)
    dbg_out("x0dbg", [128, S], BF16)
    dbg_out("qxT", [128, 16 * S], FP8)
    dbg_out("tsc", [S, IFF], BF16)
    dbg_out("qtT", [128, 44 * S], FP8)

    es_root = ExitStack()
    sbA = lambda n, sh, dt: es_root.enter_context(nc.sbuf_tensor(n, list(sh), dt))
    prog = Prog(nc, es_root)
    c_t = prog.csem["tensor"]
    c_v = prog.csem["vector"]
    c_s = prog.csem["scalar"]
    c_g = prog.csem["gpsimd"]

    B = lambda: Buf()
    bufs = []
    def NB(n=1):
        r = [Buf() for _ in range(n)]
        bufs.extend(r)
        return r if n > 1 else r[0]

    trib = sbA("trib", [128, 128], BF16); b_tri = NB()
    idq = sbA("idq", [128, 128], FP8); b_idq = NB()
    idf = sbA("idf", [128, 128], F32); b_idf = NB()
    rmsb = sbA("rmsb", [128, 1], F32); b_rms = NB()
    lamb = sbA("lamb", [128, 16], F32); b_lam = NB()
    idxb = sbA("idxb", [128, 44], I32); b_idx = NB()
    sc = {k: sbA("sc_" + k, [128, max(NT, 16)], F32) for k in
          ("t1", "t2", "m", "r", "nm", "sx")}
    b_sc = {k: NB() for k in sc}
    eps5 = sbA("eps5", [128, 1], F32); b_e5 = NB()
    eps8 = sbA("eps8", [128, 1], F32); b_e8 = NB()
    wdmax = sbA("wdmax", [128, 16], F32); b_wdm = NB()
    wdsc = sbA("wdsc", [128, 16], F32); b_wds = NB()
    tmaxb = sbA("tmaxb", [128, NT], F32); b_tmx = NB()
    tscb = sbA("tscb", [128, NT], F32); b_tsc = NB()
    qscb = sbA("qscb", [128, NT], F32); b_qsc = NB()

    b_qs, b_xm, b_ts, b_ys, b_out = NB(), NB(), NB(), NB(), NB()
    b_x = NB()
    prog.dma("sync", trib[:], tri_t.ap(), wr=[b_tri])
    prog.op("vector", lambda e: e.memset(eps5[:], 1e-5), wr=[b_e5])
    prog.op("vector", lambda e: e.memset(eps8[:], 1e-8), wr=[b_e8])
    prog.dma("sync", idq[:], identq_t.ap(), wr=[b_idq])
    prog.dma("sync", idf[:], identf_t.ap(), wr=[b_idf])
    prog.dma("sync", rmsb[:], rmsw.ap(), wr=[b_rms])
    prog.dma("sync", idxb[:], idx_t.ap().rearrange("(c p) o -> p (c o)", p=128),
             wr=[b_idx], allow_slow_non_contiguous=True)

    lqb = sbA("lqb", [128, 512], F32); b_lq = NB()
    for i, t in enumerate([lq1, lq2, lk1, lk2]):
        prog.dma("sync", lqb[0:16, i * 128:(i + 1) * 128], t.ap(), wr=[b_lq])
    lsc1 = sbA("lsc1", [128, 128], F32); b_l1 = NB()
    lsc2 = sbA("lsc2", [16, 2], F32); b_l2 = NB()
    lsc3 = sbA("lsc3", [16, 2], F32); b_l3 = NB()
    lam16 = sbA("lam16", [16, 1], F32); b_l4 = NB()
    lrow = sbA("lrow", [1, 16], F32); b_l5 = NB()
    onesc = sbA("onesc", [1, 128], F32); b_l6 = NB()

    esAB = ExitStack()
    KT = esAB.enter_context(nc.sbuf_tensor("KT", [128, 8 * S], BF16)); b_KT = NB()
    Vx = esAB.enter_context(nc.sbuf_tensor("Vx", [128, KVH * NT * 132], BF16))
    b_V = NB()
    esA = ExitStack()
    sbP = lambda n, sh, dt: esA.enter_context(nc.sbuf_tensor(n, list(sh), dt))
    cosb = sbP("cosb", [128, NT * 256], BF16); b_cos = NB()
    sinb = sbP("sinb", [128, NT * 256], BF16); b_sin = NB()
    lnxT = sbP("lnxT", [128, 16 * S], BF16); b_lnT = NB(NT) if NT > 1 else [NB()]
    xst = [sbP(f"xst{i}", [128, E], F32) for i in range(2)]; b_xst = NB(2)
    sqscr = sbP("sqscr", [128, E], F32); b_sq = NB()
    lnst = [sbP(f"lnst{i}", [128, E], BF16) for i in range(2)]; b_lns = NB(2)
    wst = [sbP(f"wst{i}", [128, 8 * 512], BF16) for i in range(2)]; b_wst = NB(2)
    qro = [sbP(f"qro{i}", [128, 512], BF16) for i in range(2)]; b_qro = NB(2)
    qtr = [sbP(f"qtr{i}", [128, 512], BF16) for i in range(2)]; b_qtr = NB(2)
    pA = [esA.enter_context(nc.psum_tensor(f"pA{i}", [128, 512], F32))
          for i in range(8)]
    b_pA = NB(8)

    prog.op("vector", lambda e: e.tensor_mul(lsc1[0:16, :], lqb[0:16, 0:128],
                                             lqb[0:16, 256:384]),
            rd=[b_lq], wr=[b_l1])
    prog.op("vector", lambda e: e.reduce_sum(lsc2[:, 0:1], lsc1[0:16, :], AX.X),
            rd=[b_l1], wr=[b_l2])
    prog.op("vector", lambda e: e.tensor_mul(lsc1[0:16, :], lqb[0:16, 128:256],
                                             lqb[0:16, 384:512]),
            rd=[b_lq], wr=[b_l1])
    prog.op("vector", lambda e: e.reduce_sum(lsc2[:, 1:2], lsc1[0:16, :], AX.X),
            rd=[b_l1], wr=[b_l2])
    prog.op("scalar", lambda e: e.activation(lsc3[:, :], lsc2[:, :], AF.Exp),
            rd=[b_l2], wr=[b_l3])
    prog.op("vector", lambda e: e.tensor_sub(lam16[:, :], lsc3[:, 0:1], lsc3[:, 1:2]),
            rd=[b_l3], wr=[b_l4])
    prog.op("vector", lambda e: e.tensor_scalar_add(lam16[:, :], lam16[:, :], LAM_INIT),
            rd=[b_l4], wr=[b_l4])
    prog.op("tensor", lambda e: e.transpose(pA[0][0:1, 0:16], lam16[:, :],
                                            idf[0:16, 0:16]),
            rd=[b_l4, b_idf], wr=[b_pA[0]])
    prog.op("vector", lambda e: e.tensor_copy(lrow[:, :], pA[0][0:1, 0:16]),
            rd=[b_pA[0]], wr=[b_l5])
    prog.op("vector", lambda e: e.memset(onesc[:], 1.0), wr=[b_l6])
    prog.op("tensor", lambda e: e.matmul(pA[0][:, 0:16], onesc[:], lrow[:, :],
                                         start=True, stop=True),
            rd=[b_l5, b_l6], wr=[b_pA[0]])
    prog.op("vector", lambda e: e.tensor_copy(lamb[:], pA[0][:, 0:16]),
            rd=[b_pA[0]], wr=[b_lam])

    prog.dma("sync", cosb[:].rearrange("p (t f) -> p t f", f=256),
             costab.ap().rearrange("(t p) f -> p t f", p=128), wr=[b_cos])
    prog.dma("sync", sinb[:].rearrange("p (t f) -> p t f", f=256),
             sintab.ap().rearrange("(t p) f -> p t f", p=128), wr=[b_sin])
    prog.op("vector", lambda e: e.memset(
        Vx[:].rearrange("p (s f) -> p s f", f=132)[:, :, 128:132], 1.0), wr=[b_V])

    for nt in range(NT):
        b = nt % 2
        prog.dma("sync", xst[b][:], x_in.ap()[nt * 128:(nt + 1) * 128, :],
                 rd=[b_x], wr=[b_xst[b]])
        prog.op("vector", lambda e, b=b, nt=nt: e.reduce_sum(
            sc["t1"][:, nt:nt + 1], xst[b][:], AX.X),
            rd=[b_xst[b]], wr=[b_sc["t1"]])
        prog.op("scalar", lambda e, b=b, nt=nt: e.activation(
            sqscr[:], xst[b][:], AF.Square, accum_out=sc["t2"][:, nt:nt + 1]),
            rd=[b_xst[b]], wr=[b_sq, b_sc["t2"]])
        prog.op("vector", lambda e, nt=nt: e.tensor_scalar_mul(
            sc["m"][:, nt:nt + 1], sc["t1"][:, nt:nt + 1], 1.0 / E),
            rd=[b_sc["t1"]], wr=[b_sc["m"]])
        prog.op("vector", lambda e, nt=nt: e.tensor_mul(
            sc["r"][:, nt:nt + 1], sc["m"][:, nt:nt + 1], sc["m"][:, nt:nt + 1]),
            rd=[b_sc["m"]], wr=[b_sc["r"]])
        prog.op("vector", lambda e, nt=nt: e.scalar_tensor_tensor(
            sc["r"][:, nt:nt + 1], sc["t2"][:, nt:nt + 1], 1.0 / E,
            sc["r"][:, nt:nt + 1], OP.mult, OP.subtract),
            rd=[b_sc["t2"], b_sc["r"]], wr=[b_sc["r"]])
        prog.op("scalar", lambda e, nt=nt: e.activation(
            sc["r"][:, nt:nt + 1], sc["r"][:, nt:nt + 1], AF.Sqrt, bias=eps5[:]),
            rd=[b_sc["r"]], rdp=[b_e5], wr=[b_sc["r"]])
        prog.op("vector", lambda e, nt=nt: e.reciprocal(
            sc["r"][:, nt:nt + 1], sc["r"][:, nt:nt + 1]),
            rd=[b_sc["r"]], wr=[b_sc["r"]])
        prog.op("vector", lambda e, nt=nt: e.scalar_tensor_tensor(
            sc["nm"][:, nt:nt + 1], sc["m"][:, nt:nt + 1], -1.0,
            sc["r"][:, nt:nt + 1], OP.mult, OP.mult),
            rd=[b_sc["m"], b_sc["r"]], wr=[b_sc["nm"]])
        prog.op("scalar", lambda e, b=b, nt=nt: e.activation(
            lnst[b][:], xst[b][:], AF.Identity, bias=sc["nm"][:, nt:nt + 1],
            scale=sc["r"][:, nt:nt + 1]),
            rd=[b_xst[b]], rdp=[b_sc["nm"], b_sc["r"]], wr=[b_lns[b]])
        prog.trans(lnxT[:].rearrange("p (ct n) -> p ct n", n=S)
                   [:, :, nt * 128:(nt + 1) * 128],
                   lnst[b][:].rearrange("p (ct f) -> p ct f", f=128),
                   rd=[b_lns[b]], wr=[b_lnT[nt]])

    jchunks = [("q", j, wq) for j in range(8)] + \
              [("k", j, wk) for j in range(2)] + [("v", 0, wv)]
    groups = [list(range(0, min(8, NT)))]
    if NT > 8:
        groups.append(list(range(8, NT)))

    for grp in groups:
        for jci, (mat, jc, wsrc) in enumerate(jchunks):
            for half8 in range(2):
                bw = (2 * jci + half8) % 2
                prog.dma("gpsimd",
                         wst[bw][:].rearrange("p (t f) -> p t f", f=512),
                         wsrc.ap()[:, jc * 512:(jc + 1) * 512].rearrange(
                             "(t p) f -> p t f", p=128)[:, half8 * 8:half8 * 8 + 8, :],
                         wr=[b_wst[bw]])
                for ct8 in range(8):
                    ct = half8 * 8 + ct8
                    for i, nt in enumerate(grp):
                        prog.op("tensor", lambda e, ct=ct, ct8=ct8, nt=nt, i=i,
                                bw=bw: e.matmul(
                            pA[i][:],
                            lnxT[:, ct * S + nt * 128: ct * S + nt * 128 + 128],
                            wst[bw][:, ct8 * 512:(ct8 + 1) * 512],
                            start=(ct == 0), stop=(ct == 15)),
                            rd=[b_lnT[nt], b_wst[bw]],
                            wr=[b_pA[i]] if ct in (0, 15) else ())
            for i, nt in enumerate(grp):
                bq = i % 2
                if mat == "v":
                    dstv = Vx[:].rearrange("p (s f) -> p s f", f=132) \
                        .rearrange("p (g t) f -> p g t f", g=KVH)[:, :, nt, 0:128]
                    prog.op("scalar", lambda e, i=i, dstv=dstv: e.activation(
                        dstv, pA[i][:].rearrange("p (g f) -> p g f", g=4), AF.Copy),
                        rd=[b_pA[i]], wr=[b_V])
                    continue
                tb_c = cosb[:, nt * 256:(nt + 1) * 256].rearrange("p (b f) -> p b f", b=4)
                tb_s = sinb[:, nt * 256:(nt + 1) * 256].rearrange("p (b f) -> p b f", b=4)
                src = pA[i][:].rearrange("p (b h f) -> p b h f", b=4, h=2)
                dstr = qro[bq][:].rearrange("p (b h f) -> p b h f", b=4, h=2)
                tmp = qtr[bq][:].rearrange("p (b h f) -> p b h f", b=4, h=2)
                re_, im_ = src[:, :, 0, :], src[:, :, 1, :]
                ore, oim = dstr[:, :, 0, :], dstr[:, :, 1, :]
                t1a, t1b = tmp[:, :, 0, :], tmp[:, :, 1, :]
                prog.op("vector", lambda e, a=t1a, x=im_, t=tb_s: e.tensor_mul(a, x, t),
                        rd=[b_pA[i], b_sin], wr=[b_qtr[bq]])
                prog.op("vector", lambda e, a=t1b, x=re_, t=tb_s: e.tensor_mul(a, x, t),
                        rd=[b_pA[i], b_sin], wr=[b_qtr[bq]])
                prog.op("vector", lambda e, a=ore, x=re_, t=tb_c: e.tensor_mul(a, x, t),
                        rd=[b_pA[i], b_cos], wr=[b_qro[bq]])
                prog.op("vector", lambda e, a=oim, x=im_, t=tb_c: e.tensor_mul(a, x, t),
                        rd=[b_pA[i], b_cos], wr=[b_qro[bq]])
                prog.op("vector", lambda e, a=ore, bb=t1a: e.tensor_sub(a, a, bb),
                        rd=[b_qtr[bq], b_qro[bq]], wr=[b_qro[bq]])
                prog.op("vector", lambda e, a=oim, bb=t1b: e.tensor_add(a, a, bb),
                        rd=[b_qtr[bq], b_qro[bq]], wr=[b_qro[bq]])
                if mat == "q":
                    prog.trans(qtr[bq][:].rearrange("p (t f) -> p t f", f=128),
                               qro[bq][:].rearrange("p (t f) -> p t f", f=128),
                               rd=[b_qro[bq]], wr=[b_qtr[bq]])
                    off = (nt * 8 + jc) * 512
                    prog.dma("sync",
                             qs.ap()[off:off + 512, :].rearrange(
                                 "(t p) f -> p t f", p=128),
                             qtr[bq][:].rearrange("p (t f) -> p t f", f=128),
                             rd=[b_qtr[bq]], wr=[b_qs])
                else:
                    dstk = KT[:].rearrange("p (bb n) -> p bb n", n=S)[
                        :, 4 * jc:4 * jc + 4, nt * 128:(nt + 1) * 128]
                    prog.trans(dstk, qro[bq][:].rearrange("p (t f) -> p t f", f=128),
                               rd=[b_qro[bq]], wr=[b_KT])

    if "lnxT" in dbg_t:
        prog.dma("sync", dbg_t["lnxT"].ap(), lnxT[:, :], rd=[b_lnT], wr=[NB()])
    if "KT" in dbg_t:
        prog.dma("sync", dbg_t["KT"].ap(), KT[:, :], rd=[b_KT], wr=[NB()])
    if "V" in dbg_t:
        prog.dma("sync", dbg_t["V"].ap(), Vx[:, :], rd=[b_V], wr=[NB()])
    if "qs" in dbg_t:
        prog.dma("sync", dbg_t["qs"].ap(), qs.ap(), rd=[b_qs], wr=[NB()])

    if "B" not in phases:
        prog.barrier(prog.dsems + list(prog.csem.values()))
        esA.close()
        esAB.close()
        es_root.close()
        prog.emit()
        return nc, dbg_t

    prog.barrier(prog.dsems + list(prog.csem.values()))
    esA.close()

    esB = ExitStack()
    sbB = lambda n, sh, dt: esB.enter_context(nc.sbuf_tensor(n, list(sh), dt))
    wob = sbB("wob", [128, 16 * E], BF16); b_wo = NB()
    qT = [sbB(f"qT{i}", [128, 4096], BF16) for i in range(2)]; b_qT = NB(2)
    x0b = [sbB(f"x0b{i}", [128, S], BF16) for i in range(2)]; b_x0 = NB(2)
    x1b = [sbB(f"x1b{i}", [128, S], BF16) for i in range(2)]; b_x1 = NB(2)
    exb = [sbB(f"exb{i}", [128, S], BF16) for i in range(3)]; b_ex = NB(3)
    exT = [sbB(f"exT{i}", [128, 16 * 128], BF16) for i in range(3)]; b_exT = NB(3)
    attr = sbB("attr", [128, H * 128], BF16); b_ar = NB()
    attT = sbB("attT", [128, H * 128], BF16); b_aT = NB()
    xinb = [sbB(f"xinb{i}", [128, E], F32) for i in range(2)]
    b_xin = NB(2)
    xmidb = sbB("xmidb", [128, E], F32); b_xmb = NB()
    sqs2 = [sbB(f"sqs2_{i}", [128, 128], BF16) for i in range(2)]; b_sq2 = NB(2)
    sqsq = sbB("sqsq", [128, 128], BF16); b_sqq = NB()
    z0p = [sbB(f"z0p{i}", [128, 8], F32) for i in range(2)]; b_z0p = NB(2)
    z1p = [sbB(f"z1p{i}", [128, 8], F32) for i in range(2)]; b_z1p = NB(2)
    zsc = [sbB(f"zsc{i}", [128, 6], F32) for i in range(2)]; b_zsc = NB(2)
    m2b = sbB("m2b", [128, 16], F32); b_m2 = NB()
    zdb = sbB("zdb", [128, 16], F32); b_zd = NB()
    stot = sbB("stot", [128, 16], F32); b_st = NB()
    st1 = sbB("st1", [128, 16], F32); b_st1 = NB()
    pE = [esB.enter_context(nc.psum_tensor(f"pE{i}", [128, 1024], F32))
          for i in range(2)]
    b_pE = NB(2)
    pATT = [esB.enter_context(nc.psum_tensor(f"pATT{i}", [128, 512], F32))
            for i in range(2)]
    b_pAT = NB(2)
    pWO = [esB.enter_context(nc.psum_tensor(f"pWO{i}", [128, 512], F32))
           for i in range(2)]
    b_pW = NB(2)

    for et in range(16):
        prog.dma("gpsimd", wob[:, et * E:(et + 1) * E],
                 wo_t.ap()[et * 128:(et + 1) * 128, :], wr=[b_wo])
        prog.op("vector", lambda e, et=et: e.tensor_scalar_mul(
            wob[:, et * E:(et + 1) * E], wob[:, et * E:(et + 1) * E], rmsb[:]),
            rd=[b_wo], rdp=[b_rms], wr=[b_wo])

    pend_wo = [None]
    for nt in range(NT):
        bq = nt % 2
        prog.dma("sync", qT[bq][:].rearrange("p (t f) -> p t f", f=128),
                 qs.ap()[nt * 4096:(nt + 1) * 4096, :].rearrange(
                     "(t p) f -> p t f", p=128), rd=[b_qs], wr=[b_qT[bq]])
        prog.dma("sync", xinb[bq][:], x_in.ap()[nt * 128:(nt + 1) * 128, :],
                 rd=[b_x], wr=[b_xin[bq]])
        def stage1(h):
            g = h // 4
            hp = h % 2
            b_zs = b_zsc[hp]
            zs = zsc[hp]
            kch = [(k0, min(KCH, nt * 128 - k0))
                   for k0 in range(0, nt * 128, KCH)]
            nci = len(kch)
            cfg = []
            for (a, xout, b_xoL, zp, b_zpL) in (
                    (0, x0b, b_x0, z0p, b_z0p), (1, x1b, b_x1, z1p, b_z1p)):
                cfg.append((a, xout[hp], b_xoL[hp], zp[hp], b_zpL[hp],
                            qT[bq][:, (2 * h + a) * 128:(2 * h + a) * 128 + 128],
                            KT[:].rearrange("p (bb n) -> p bb n", n=S)[:, 2 * g + a, :],
                            pE[a], b_pE[a]))
            for ci, (k0, kw) in enumerate(kch):
                for (a, xout, b_xo, zp, b_zp, lq_ap, kt_ap, pe, b_pe) in cfg:
                    for s0 in range(0, kw, 512):
                        sw = min(512, kw - s0)
                        prog.op("tensor", lambda e, pe=pe, lq=lq_ap, kt=kt_ap,
                                k0=k0, s0=s0, sw=sw: e.matmul(
                                    pe[:, s0:s0 + sw], lq,
                                    kt[:, k0 + s0:k0 + s0 + sw],
                                    start=True, stop=True),
                                rd=[b_qT[bq], b_KT],
                                wr=[b_pe] if (s0 == 0 or s0 + sw == kw) else ())
                    prog.op("scalar", lambda e, pe=pe, xo=xout, k0=k0, kw=kw, zp=zp,
                            ci=ci: e.activation(
                                xo[:, k0:k0 + kw], pe[:, 0:kw], AF.Exp, scale=SQD,
                                accum_out=zp[:, ci:ci + 1]),
                            rd=[b_pe], wr=[b_xo, b_zp])
            for (a, xout, b_xo, zp, b_zp, lq_ap, kt_ap, pe, b_pe) in cfg:
                prog.op("tensor", lambda e, lq=lq_ap, kt=kt_ap, nt=nt, pe=pe:
                        e.matmul(
                    pe[:, 512:640], lq, kt[:, nt * 128:nt * 128 + 128],
                    start=True, stop=True), rd=[b_qT[bq], b_KT], wr=[b_pe])
                prog.op("scalar", lambda e, hp=hp, pe=pe: e.activation(
                    sqs2[hp][:], pe[:, 512:640], AF.Exp, scale=SQD),
                    rd=[b_pe], wr=[b_sq2[hp]])
                prog.op("vector", lambda e, xo=xout, nt=nt, zp=zp, nci=nci, hp=hp:
                        e.scalar_tensor_tensor(
                            xo[:, nt * 128:(nt + 1) * 128], sqs2[hp][:], 1.0, trib[:],
                            OP.mult, OP.mult, accum_out=zp[:, nci:nci + 1]),
                        rd=[b_sq2[hp], b_tri], wr=[b_xo, b_zp])


        def stage1b(h):
            g = h // 4
            hp = h % 2
            he = h % 3
            b_zs = b_zsc[hp]
            zs = zsc[hp]
            kch = [(k0, min(KCH, nt * 128 - k0))
                   for k0 in range(0, nt * 128, KCH)]
            ncht = len(kch) + 1
            prog.op("vector", lambda e, n=ncht, zs=zs, hp=hp: e.reduce_sum(
                zs[:, 0:1], z0p[hp][:, 0:n], AX.X), rd=[b_z0p[hp]], wr=[b_zs])
            prog.op("vector", lambda e, n=ncht, zs=zs, hp=hp: e.reduce_sum(
                zs[:, 1:2], z1p[hp][:, 0:n], AX.X), rd=[b_z1p[hp]], wr=[b_zs])
            prog.op("vector", lambda e, zs=zs: e.reciprocal(zs[:, 2:3], zs[:, 0:1]),
                    rd=[b_zs], wr=[b_zs])
            prog.op("vector", lambda e, zs=zs: e.reciprocal(zs[:, 3:4], zs[:, 1:2]),
                    rd=[b_zs], wr=[b_zs])
            prog.op("vector", lambda e, zs=zs: e.tensor_mul(zs[:, 3:4], zs[:, 3:4],
                                                            zs[:, 0:1]),
                    rd=[b_zs], wr=[b_zs])
            prog.op("vector", lambda e, h=h, zs=zs: e.tensor_scalar(
                zs[:, 3:4], zs[:, 3:4], lamb[:, h:h + 1], -1.0, OP.mult, OP.mult),
                rd=[b_zs], rdp=[b_lam], wr=[b_zs])
            kwa = (nt + 1) * 128
            prog.op("vector", lambda e, kw=kwa, hp=hp, zs=zs: e.scalar_tensor_tensor(
                x1b[hp][:, 0:kw], x1b[hp][:, 0:kw], zs[:, 3:4], x0b[hp][:, 0:kw],
                OP.mult, OP.add), rd=[b_x1[hp], b_x0[hp]], rdp=[b_zs],
                wr=[b_x1[hp]])
            he = h % 3
            prog.op("scalar", lambda e, kw=kwa, hp=hp, he=he, zs=zs: e.activation(
                exb[he][:, 0:kw], x1b[hp][:, 0:kw], AF.Exp, scale=zs[:, 2:3]),
                rd=[b_x1[hp]], rdp=[b_zs], wr=[b_ex[he]])
            prog.op("vector", lambda e, nt=nt, he=he: e.tensor_mul(
                exb[he][:, nt * 128:(nt + 1) * 128],
                exb[he][:, nt * 128:(nt + 1) * 128],
                trib[:]), rd=[b_ex[he], b_tri], wr=[b_ex[he]])
            if "x0dbg" in dbg_t and nt == 1 and h == 2:
                prog.dma("sync", dbg_t["x0dbg"].ap(), exb[he][:, 0:S],
                         rd=[b_ex[he]], wr=[NB()])
            prog.trans(exT[he][:].rearrange("p (t f) -> p t f", f=128)[:, 0:nt + 1, :],
                       exb[he][:, 0:kwa].rearrange("p (t f) -> p t f", f=128),
                       rd=[b_ex[he]], wr=[b_exT[he]])

        def stage2(h):
            g = h // 4
            he = h % 3
            pat, b_pat = pATT[h % 2], b_pAT[h % 2]
            for kt in range(nt + 1):
                vsl = Vx[:].rearrange("p (s f) -> p s f", f=132)[:, g * NT + kt, 0:129]
                prog.op("tensor", lambda e, kt=kt, vsl=vsl, pat=pat, nt=nt, he=he:
                        e.matmul(
                    pat[:, 0:129], exT[he][:, kt * 128:(kt + 1) * 128], vsl,
                    start=(kt == 0), stop=(kt == nt)),
                    rd=[b_exT[he], b_V], wr=[b_pat] if kt in (0, nt) else ())
            prog.op("scalar", lambda e, h=h, pat=pat: e.activation(
                sqsq[:], pat[:, 0:128], AF.Square, accum_out=m2b[:, h:h + 1]),
                rd=[b_pat], wr=[b_sqq, b_m2])
            prog.op("scalar", lambda e, h=h, pat=pat: e.activation(
                attr[:, h * 128:(h + 1) * 128], pat[:, 0:128], AF.Copy),
                rd=[b_pat], wr=[b_ar])
            prog.op("vector", lambda e, h=h, pat=pat: e.tensor_copy(
                zdb[:, h:h + 1], pat[:, 128:129]), rd=[b_pat], wr=[b_zd])

        for h in range(H + 1):
            if h < H:
                stage1(h)
            if h >= 1:
                stage1b(h - 1)
                stage2(h - 1)
        def epilogue(nt=nt, bq=bq):
            prog.op("vector", lambda e: e.reciprocal(st1[:], zdb[:]),
                    rd=[b_zd], wr=[b_st1])
            prog.op("vector", lambda e: e.tensor_mul(m2b[:], m2b[:], st1[:]),
                    rd=[b_st1, b_m2], wr=[b_m2])
            prog.op("vector", lambda e: e.tensor_mul(m2b[:], m2b[:], st1[:]),
                    rd=[b_st1, b_m2], wr=[b_m2])
            prog.op("vector", lambda e: e.tensor_scalar_mul(m2b[:], m2b[:], 1.0 / D),
                    rd=[b_m2], wr=[b_m2])
            prog.op("scalar", lambda e: e.activation(m2b[:], m2b[:], AF.Sqrt, bias=eps8[:]),
                    rd=[b_m2], rdp=[b_e8], wr=[b_m2])
            prog.op("vector", lambda e: e.reciprocal(m2b[:], m2b[:]),
                    rd=[b_m2], wr=[b_m2])
            prog.op("vector", lambda e: e.scalar_tensor_tensor(
                stot[:], m2b[:], OUT_GAIN, st1[:], OP.mult, OP.mult),
                rd=[b_m2, b_st1], wr=[b_st])
            for h in range(H):
                prog.op("vector", lambda e, h=h: e.tensor_scalar_mul(
                    attr[:, h * 128:(h + 1) * 128], attr[:, h * 128:(h + 1) * 128],
                    stot[:, h:h + 1]), rd=[b_ar], rdp=[b_st], wr=[b_ar])
            prog.trans(attT[:].rearrange("p (t f) -> p t f", f=128),
                       attr[:].rearrange("p (t f) -> p t f", f=128),
                       rd=[b_ar], wr=[b_aT])


            def wo_part(nt=nt, bq=bq):
                for oc in range(4):
                    pw, b_pw = pWO[oc % 2], b_pW[oc % 2]
                    for et in range(16):
                        prog.op("tensor", lambda e, oc=oc, et=et, pw=pw: e.matmul(
                            pw[:], attT[:, et * 128:(et + 1) * 128],
                            wob[:, et * E + oc * 512: et * E + oc * 512 + 512],
                            start=(et == 0), stop=(et == 15)),
                            rd=[b_aT, b_wo], wr=[b_pw] if et in (0, 15) else ())
                    prog.op("vector", lambda e, oc=oc, pw=pw: e.scalar_tensor_tensor(
                        xmidb[:, oc * 512:(oc + 1) * 512], pw[:], 1.0,
                        xinb[bq][:, oc * 512:(oc + 1) * 512], OP.mult, OP.add),
                        rd=[b_pw, b_xin[bq]], wr=[b_xmb])
                prog.dma("sync", xm.ap()[nt * 128:(nt + 1) * 128, :], xmidb[:], rd=[b_xmb], wr=[b_xm])

            wo_part()
        epilogue()

    if pend_wo[0] is not None:
        pend_wo[0]()
        pend_wo[0] = None

    if "xmid" in dbg_t:
        prog.dma("sync", dbg_t["xmid"].ap(), xm.ap(), rd=[b_xm], wr=[NB()])

    if "C" not in phases:
        prog.barrier(prog.dsems + list(prog.csem.values()))
        esB.close()
        esAB.close()
        es_root.close()
        prog.emit()
        return nc, dbg_t

    prog.barrier(prog.dsems + list(prog.csem.values()))
    esB.close()
    esAB.close()

    esC = ExitStack()
    sbC = lambda n, sh, dt: esC.enter_context(nc.sbuf_tensor(n, list(sh), dt))
    qxT8 = sbC("qxT8", [128, 16 * S], FP8); b_qx = NB()
    xstC = [sbC(f"xstC{i}", [128, E], F32) for i in range(2)]; b_xsC = NB(2)
    sqsC = sbC("sqsC", [128, E], F32); b_sqC = NB()
    qx8s = sbC("qx8s", [128, E], FP8); b_q8 = NB()
    wgf = [sbC(f"wgf{i}", [128, E], BF16) for i in range(2)]; b_wgf = NB(2)
    wuf = [sbC(f"wuf{i}", [128, E], BF16) for i in range(2)]; b_wuf = NB(2)
    wdum = sbC("wdum", [128, E], BF16); b_wdum = NB()
    w8s = [sbC(f"w8s{i}", [128, E], FP8) for i in range(2)]; b_w8 = NB(2)
    wsc2 = [sbC(f"wsc2{i}", [128, E], BF16) for i in range(2)]; b_ws2 = NB(2)
    wgT = [sbC(f"wgT{i}", [128, 16 * 512], BF16) for i in range(3)]; b_wgT = NB(3)
    wuT = [sbC(f"wuT{i}", [128, 16 * 512], BF16) for i in range(3)]; b_wuT = NB(3)
    wm1 = sbC("wm1", [128, 16], F32)
    b_wm1 = NB(8)
    sgb = sbC("sgb", [128, 512], BF16); b_sg = NB()
    tst = sbC("tst", [128, 512], BF16); b_tst = NB()
    esF8 = ExitStack()
    pF8 = esF8.enter_context(nc.psum_tensor("pF8", [128, 8192], FP8))
    b_pF = NB()
    pF8v = pF8[:].rearrange("p (n two) -> p n two", two=2)
    pF8g = lambda n_t: pF8[:].rearrange(
        "p (it f two) -> p it f two", two=2, f=128)[:, 0:n_t, :, 0:1]
    pG = [esC.enter_context(nc.psum_tensor(f"pG{i}", [128, 512], F32))
          for i in range(4)]
    b_pG = NB(4)

    for nt in range(NT):
        b = nt % 2
        prog.dma("sync", xstC[b][:], xm.ap()[nt * 128:(nt + 1) * 128, :],
                 rd=[b_xm], wr=[b_xsC[b]])
        prog.op("vector", lambda e, b=b, nt=nt: e.reduce_sum(
            sc["t1"][:, nt:nt + 1], xstC[b][:], AX.X),
            rd=[b_xsC[b]], wr=[b_sc["t1"]])
        prog.op("scalar", lambda e, b=b, nt=nt: e.activation(
            sqsC[:], xstC[b][:], AF.Square, accum_out=sc["t2"][:, nt:nt + 1]),
            rd=[b_xsC[b]], wr=[b_sqC, b_sc["t2"]])
        prog.op("vector", lambda e, b=b, nt=nt: e.reduce_max(
            sc["m"][:, nt:nt + 1], xstC[b][:], AX.X),
            rd=[b_xsC[b]], wr=[b_sc["m"]])
        prog.op("vector", lambda e, b=b, nt=nt: e.tensor_reduce(
            sc["nm"][:, nt:nt + 1], xstC[b][:], AX.X, OP.min),
            rd=[b_xsC[b]], wr=[b_sc["nm"]])
        prog.op("vector", lambda e, nt=nt: e.tensor_scalar_mul(
            sc["t1"][:, nt:nt + 1], sc["t1"][:, nt:nt + 1], 1.0 / E),
            rd=[b_sc["t1"]], wr=[b_sc["t1"]])
        prog.op("vector", lambda e, nt=nt: e.tensor_mul(
            sc["r"][:, nt:nt + 1], sc["t1"][:, nt:nt + 1], sc["t1"][:, nt:nt + 1]),
            rd=[b_sc["t1"]], wr=[b_sc["r"]])
        prog.op("vector", lambda e, nt=nt: e.scalar_tensor_tensor(
            sc["r"][:, nt:nt + 1], sc["t2"][:, nt:nt + 1], 1.0 / E,
            sc["r"][:, nt:nt + 1], OP.mult, OP.subtract),
            rd=[b_sc["t2"], b_sc["r"]], wr=[b_sc["r"]])
        prog.op("scalar", lambda e, nt=nt: e.activation(
            sc["r"][:, nt:nt + 1], sc["r"][:, nt:nt + 1], AF.Sqrt, bias=eps5[:]),
            rd=[b_sc["r"]], rdp=[b_e5], wr=[b_sc["r"]])
        prog.op("vector", lambda e, nt=nt: e.reciprocal(
            sc["r"][:, nt:nt + 1], sc["r"][:, nt:nt + 1]),
            rd=[b_sc["r"]], wr=[b_sc["r"]])
        prog.op("vector", lambda e, nt=nt: e.tensor_sub(
            sc["m"][:, nt:nt + 1], sc["m"][:, nt:nt + 1], sc["t1"][:, nt:nt + 1]),
            rd=[b_sc["m"], b_sc["t1"]], wr=[b_sc["m"]])
        prog.op("vector", lambda e, nt=nt: e.tensor_sub(
            sc["nm"][:, nt:nt + 1], sc["t1"][:, nt:nt + 1], sc["nm"][:, nt:nt + 1]),
            rd=[b_sc["nm"], b_sc["t1"]], wr=[b_sc["nm"]])
        prog.op("vector", lambda e, nt=nt: e.tensor_max(
            sc["m"][:, nt:nt + 1], sc["m"][:, nt:nt + 1], sc["nm"][:, nt:nt + 1]),
            rd=[b_sc["m"], b_sc["nm"]], wr=[b_sc["m"]])
        prog.op("vector", lambda e, nt=nt: e.tensor_mul(
            sc["sx"][:, nt:nt + 1], sc["m"][:, nt:nt + 1], sc["r"][:, nt:nt + 1]),
            rd=[b_sc["m"], b_sc["r"]], wr=[b_sc["sx"]])
        prog.op("vector", lambda e, nt=nt: e.tensor_scalar_mul(
            qscb[:, nt:nt + 1], sc["sx"][:, nt:nt + 1], 1.0 / 224.0),
            rd=[b_sc["sx"]], wr=[b_qsc])
        prog.op("vector", lambda e, nt=nt: e.reciprocal(
            sc["t2"][:, nt:nt + 1], sc["sx"][:, nt:nt + 1]),
            rd=[b_sc["sx"]], wr=[b_sc["t2"]])
        prog.op("vector", lambda e, nt=nt: e.scalar_tensor_tensor(
            sc["t2"][:, nt:nt + 1], sc["t2"][:, nt:nt + 1], 224.0,
            sc["r"][:, nt:nt + 1], OP.mult, OP.mult),
            rd=[b_sc["r"], b_sc["t2"]], wr=[b_sc["t2"]])
        prog.op("vector", lambda e, nt=nt: e.scalar_tensor_tensor(
            sc["nm"][:, nt:nt + 1], sc["t1"][:, nt:nt + 1], -1.0,
            sc["t2"][:, nt:nt + 1], OP.mult, OP.mult),
            rd=[b_sc["t1"], b_sc["t2"]], wr=[b_sc["nm"]])
        prog.op("scalar", lambda e, b=b, nt=nt: e.activation(
            qx8s[:], xstC[b][:], AF.Identity, bias=sc["nm"][:, nt:nt + 1],
            scale=sc["t2"][:, nt:nt + 1]),
            rd=[b_xsC[b]], rdp=[b_sc["nm"], b_sc["t2"]], wr=[b_q8])
        for ct in range(16):
            prog.op("tensor", lambda e, ct=ct: e.transpose(
                pF8v[:, ct * 128:(ct + 1) * 128, 0:1],
                qx8s[:, ct * 128:(ct + 1) * 128],
                idq[:]), rd=[b_q8, b_idq], wr=[b_pF] if ct in (0, 15) else ())
        prog.op("scalar", lambda e, nt=nt: e.activation(
            qxT8[:].rearrange("p (ct n) -> p ct n", n=S)[:, :, nt * 128:(nt + 1) * 128],
            pF8g(16), AF.Copy),
            rd=[b_pF], wr=[b_qx])
    if "qxT" in dbg_t:
        prog.dma("sync", dbg_t["qxT"].ap(), qxT8[:, :], rd=[b_qx], wr=[NB()])

    def prep_ic(ic):
        bw = ic % 3
        for (wf, b_wf, wsrc, wtb, b_wtb) in (
                (wgf, b_wgf, wg_t, wgT, b_wgT), (wuf, b_wuf, wu_t, wuT, b_wuT)):
            for halfq in range(4):
                ich = ic * 4 + halfq
                mi = (0 if wf is wgf else 4) + halfq
                b_m = b_wm1[mi]
                sp = halfq % 2
                prog.op("gpsimd", lambda e, sp=sp, wf=wf, wsrc=wsrc, ich=ich:
                        e.indirect_dma_start(
                            wf[sp][:], None, wsrc.ap(),
                            bass.IndirectOffsetOnAxis(idxb[:, ich:ich + 1], 0)),
                        rd=[b_idx], wr=[b_wf[sp]],
                        sem=prog.buf_dsem(b_wf[sp]), inck=16)
                prog.op("vector", lambda e, sp=sp, wf=wf, mi=mi: e.reduce_max(
                    wm1[:, 2 * mi:2 * mi + 1], wf[sp][:], AX.X,
                    apply_absolute_value=True),
                    rd=[b_wf[sp]], wr=[b_m])
                prog.op("vector", lambda e, mi=mi: e.reciprocal(
                    wm1[:, 2 * mi + 1:2 * mi + 2], wm1[:, 2 * mi:2 * mi + 1]),
                    rd=[b_m], wr=[b_m])
                prog.op("vector", lambda e, mi=mi: e.tensor_scalar_mul(
                    wm1[:, 2 * mi + 1:2 * mi + 2], wm1[:, 2 * mi + 1:2 * mi + 2],
                    224.0), rd=[b_m], wr=[b_m])
                prog.op("vector", lambda e, mi=mi: e.tensor_scalar_mul(
                    wm1[:, 2 * mi:2 * mi + 1], wm1[:, 2 * mi:2 * mi + 1],
                    1.0 / 224.0), rd=[b_m], wr=[b_m])
                prog.op("scalar", lambda e, sp=sp, wf=wf, mi=mi: e.activation(
                    w8s[sp][:], wf[sp][:], AF.Identity,
                    scale=wm1[:, 2 * mi + 1:2 * mi + 2]),
                    rd=[b_wf[sp]], rdp=[b_m], wr=[b_w8[sp]])
                prog.op("vector", lambda e, sp=sp, mi=mi: e.tensor_scalar_mul(
                    wsc2[sp][:], w8s[sp][:], wm1[:, 2 * mi:2 * mi + 1]),
                    rd=[b_w8[sp]], rdp=[b_m], wr=[b_ws2[sp]])
                prog.trans(wtb[bw][:].rearrange("p (ct f) -> p ct f", f=512)
                           [:, :, halfq * 128:(halfq + 1) * 128],
                           wsc2[sp][:].rearrange("p (ct f) -> p ct f", f=128),
                           rd=[b_ws2[sp]], wr=[b_wtb[bw]])

    def mm_ic(ic):
        bw = ic % 3
        for nt in range(NT):
            pg, b_pg = pG[2 * (nt % 2)], b_pG[2 * (nt % 2)]
            pu, b_pu = pG[2 * (nt % 2) + 1], b_pG[2 * (nt % 2) + 1]
            for ct in range(16):
                prog.op("tensor", lambda e, pg=pg, ct=ct, nt=nt, bw=bw: e.matmul(
                    pg[:], qxT8[:, ct * S + nt * 128: ct * S + nt * 128 + 128],
                    wgT[bw][:, ct * 512:(ct + 1) * 512],
                    start=(ct == 0), stop=(ct == 15)),
                    rd=[b_qx, b_wgT[bw]], wr=[b_pg] if ct in (0, 15) else ())
                prog.op("tensor", lambda e, pu=pu, ct=ct, nt=nt, bw=bw: e.matmul(
                    pu[:], qxT8[:, ct * S + nt * 128: ct * S + nt * 128 + 128],
                    wuT[bw][:, ct * 512:(ct + 1) * 512],
                    start=(ct == 0), stop=(ct == 15)),
                    rd=[b_qx, b_wuT[bw]], wr=[b_pu] if ct in (0, 15) else ())
            prog.op("scalar", lambda e, pg=pg, nt=nt: e.activation(
                sgb[:], pg[:], AF.Silu, scale=qscb[:, nt:nt + 1]),
                rd=[b_pg], rdp=[b_qsc], wr=[b_sg])
            prog.op("vector", lambda e, pu=pu, nt=nt: e.scalar_tensor_tensor(
                tst[:], pu[:], qscb[:, nt:nt + 1], sgb[:], OP.mult, OP.mult),
                rd=[b_pu, b_sg], rdp=[b_qsc], wr=[b_tst])
            prog.dma("sync", tsc.ap()[nt * 128:(nt + 1) * 128,
                                      ic * 512:(ic + 1) * 512],
                     tst[:], rd=[b_tst], wr=[b_ts])

    prep_ic(0)
    prep_ic(1)
    for ic in range(11):
        if ic + 2 < 11:
            prep_ic(ic + 2)
        mm_ic(ic)
    if "tsc" in dbg_t:
        prog.dma("sync", dbg_t["tsc"].ap(), tsc.ap(), rd=[b_ts], wr=[NB()])

    prog.barrier(prog.dsems + list(prog.csem.values()))
    esC.close()
    esC2 = ExitStack()
    sbC2 = lambda n, sh, dt: esC2.enter_context(nc.sbuf_tensor(n, list(sh), dt))
    qtT8 = sbC2("qtT8", [128, 44 * S], FP8); b_qt = NB()
    esC2b = ExitStack()
    sbC2b = lambda n, sh, dt: esC2b.enter_context(nc.sbuf_tensor(n, list(sh), dt))
    tld = [sbC2b(f"tld{i}", [128, IFF], BF16) for i in range(2)]; b_tld = NB(2)
    qt8s = sbC2b("qt8s", [128, IFF], FP8); b_qt8 = NB()
    tdum = sbC2b("tdum", [128, IFF], BF16); b_tdum = NB()

    for nt in range(NT):
        b = nt % 2
        prog.dma("sync", tld[b][:], tsc.ap()[nt * 128:(nt + 1) * 128, :],
                 rd=[b_ts], wr=[b_tld[b]])
        prog.op("vector", lambda e, b=b, nt=nt: e.reduce_max(
            tmaxb[:, nt:nt + 1], tld[b][:], AX.X, apply_absolute_value=True),
            rd=[b_tld[b]], wr=[b_tmx])
        prog.op("vector", lambda e, nt=nt: e.reciprocal(
            sc["t1"][:, nt:nt + 1], tmaxb[:, nt:nt + 1]),
            rd=[b_tmx], wr=[b_sc["t1"]])
        prog.op("vector", lambda e, nt=nt: e.tensor_scalar_mul(
            sc["t1"][:, nt:nt + 1], sc["t1"][:, nt:nt + 1], 224.0),
            rd=[b_sc["t1"]], wr=[b_sc["t1"]])
        prog.op("vector", lambda e, nt=nt: e.tensor_scalar_mul(
            tscb[:, nt:nt + 1], tmaxb[:, nt:nt + 1], 1.0 / 224.0),
            rd=[b_tmx], wr=[b_tsc])
        prog.op("scalar", lambda e, b=b, nt=nt: e.activation(
            qt8s[:], tld[b][:], AF.Identity, scale=sc["t1"][:, nt:nt + 1]),
            rd=[b_tld[b]], rdp=[b_sc["t1"]], wr=[b_qt8])
        for hq in range(2):
            ntl = 22
            for it in range(ntl):
                itg = hq * 22 + it
                prog.op("tensor", lambda e, itg=itg, it=it: e.transpose(
                    pF8v[:, it * 128:(it + 1) * 128, 0:1],
                    qt8s[:, itg * 128:(itg + 1) * 128], idq[:]),
                    rd=[b_qt8, b_idq], wr=[b_pF] if it in (0, ntl - 1) else ())
            prog.op("scalar", lambda e, hq=hq, nt=nt, ntl=ntl: e.activation(
                qtT8[:].rearrange("p (it n) -> p it n", n=S)
                [:, hq * 22:hq * 22 + ntl, nt * 128:(nt + 1) * 128],
                pF8g(ntl), AF.Copy),
                rd=[b_pF], wr=[b_qt])
    if "qtT" in dbg_t:
        prog.dma("sync", dbg_t["qtT"].ap(), qtT8[:, :], rd=[b_qt], wr=[NB()])

    if "D" not in phases:
        prog.barrier(prog.dsems + list(prog.csem.values()))
        esC2.close()
        esF8.close()
        es_root.close()
        prog.emit()
        return nc, dbg_t

    prog.barrier(prog.dsems + list(prog.csem.values()))
    esC2b.close()
    esD = ExitStack()
    sbD = lambda n, sh, dt: esD.enter_context(nc.sbuf_tensor(n, list(sh), dt))
    qwdT = [sbD(f"qwdT{i}", [128, 44 * 128], FP8) for i in range(2)]; b_qw = NB(2)
    wdf = [sbD(f"wdf{i}", [128, IFF], F32) for i in range(2)]; b_wdD = NB(2)
    wd8s = [sbD(f"wd8s{i}", [128, 512], FP8) for i in range(4)]; b_w8D = NB(4)
    wdr = [sbD(f"wdr{i}", [128, 1], F32) for i in range(2)]; b_wdr = NB(2)
    ytst = sbD("ytst", [128, S], BF16); b_yt = NB()
    ytT = sbD("ytT", [128, 16 * 128], BF16); b_yT = NB()
    pY = [esD.enter_context(nc.psum_tensor(f"pY{i}", [128, 512], F32))
          for i in range(2)]
    b_pY = NB(2)

    def prepD(ht):
        bh = ht % 2
        prog.dma("sync", wdf[bh][:], wd_t.ap()[ht * 128:(ht + 1) * 128, :],
                 wr=[b_wdD[bh]])
        prog.op("vector", lambda e, bh=bh, ht=ht: e.reduce_max(
            wdmax[:, ht:ht + 1], wdf[bh][:], AX.X, apply_absolute_value=True),
            rd=[b_wdD[bh]], wr=[b_wdm])
        prog.op("vector", lambda e, bh=bh, ht=ht: e.reciprocal(
            wdr[bh][:], wdmax[:, ht:ht + 1]), rd=[b_wdm], wr=[b_wdr[bh]])
        prog.op("vector", lambda e, ht=ht: e.tensor_scalar_mul(
            wdsc[:, ht:ht + 1], wdmax[:, ht:ht + 1], 1.0 / 224.0),
            rd=[b_wdm], wr=[b_wds])
        prog.op("vector", lambda e, bh=bh: e.tensor_scalar_mul(
            wdr[bh][:], wdr[bh][:], 224.0), rd=[b_wdr[bh]], wr=[b_wdr[bh]])
        for icc in range(11):
            bw = 2 * bh + icc % 2
            prog.op("vector", lambda e, bh=bh, bw=bw, icc=icc: e.tensor_scalar_mul(
                wd8s[bw][:], wdf[bh][:, icc * 512:(icc + 1) * 512], wdr[bh][:]),
                rd=[b_wdD[bh]], rdp=[b_wdr[bh]], wr=[b_w8D[bw]])
            for t4 in range(4):
                prog.op("tensor", lambda e, bw=bw, t4=t4: e.transpose(
                    pF8v[:, t4 * 128:(t4 + 1) * 128, 0:1],
                    wd8s[bw][:, t4 * 128:(t4 + 1) * 128],
                    idq[:]), rd=[b_w8D[bw], b_idq],
                    wr=[b_pF] if t4 in (0, 3) else ())
            prog.op("vector", lambda e, bh=bh, icc=icc: e.tensor_copy(
                qwdT[bh][:, icc * 512:(icc + 1) * 512].rearrange(
                    "p (t f) -> p t f", f=128),
                pF8g(4)[:, :, :, 0]), rd=[b_pF], wr=[b_qw[bh]])

    def mmD(ht):
        bh = ht % 2
        nchunks = [(n0, min(512, S - n0)) for n0 in range(0, S, 512)]
        for ncc, (n0, nw) in enumerate(nchunks):
            py, b_py = pY[ncc % 2], b_pY[ncc % 2]
            for ic44 in range(44):
                prog.op("tensor", lambda e, ic44=ic44, n0=n0, nw=nw, py=py, bh=bh:
                        e.matmul(
                    py[:, 0:nw], qwdT[bh][:, ic44 * 128:(ic44 + 1) * 128],
                    qtT8[:, ic44 * S + n0: ic44 * S + n0 + nw],
                    start=(ic44 == 0), stop=(ic44 == 43)),
                    rd=[b_qw[bh], b_qt], wr=[b_py] if ic44 in (0, 43) else ())
            prog.op("vector", lambda e, n0=n0, nw=nw, ht=ht, py=py: e.tensor_scalar_mul(
                ytst[:, n0:n0 + nw], py[:, 0:nw], wdsc[:, ht:ht + 1]),
                rd=[b_py], rdp=[b_wds], wr=[b_yt])
        prog.trans(ytT[:, 0:NT * 128].rearrange("p (t f) -> p t f", f=128),
                   ytst[:, 0:S].rearrange("p (t f) -> p t f", f=128),
                   rd=[b_yt], wr=[b_yT])
        prog.dma("sync", ysc.ap()[:, ht * 128:(ht + 1) * 128]
                 .rearrange("(t p) f -> p t f", p=128),
                 ytT[:, 0:NT * 128].rearrange("p (t f) -> p t f", f=128),
                 rd=[b_yT], wr=[b_ys])

    prepD(0)
    for ht in range(16):
        if ht + 1 < 16:
            prepD(ht + 1)
        mmD(ht)

    yld = [sbD("yld0", [128, E], BF16)] * 2; b_yl0 = NB()
    b_yl = [b_yl0, b_yl0]
    xmld = [sbD("xmld0", [128, E], F32)] * 2; b_xl0 = NB()
    b_xl = [b_xl0, b_xl0]
    outst = [sbD("outst0", [128, E], F32)] * 2; b_os0 = NB()
    b_os = [b_os0, b_os0]
    for nt in range(NT):
        b = nt % 2
        prog.dma("sync", yld[b][:], ysc.ap()[nt * 128:(nt + 1) * 128, :],
                 rd=[b_ys], wr=[b_yl[b]])
        prog.dma("sync", xmld[b][:], xm.ap()[nt * 128:(nt + 1) * 128, :],
                 rd=[b_xm], wr=[b_xl[b]])
        prog.op("vector", lambda e, b=b, nt=nt: e.scalar_tensor_tensor(
            outst[b][:], yld[b][:], tscb[:, nt:nt + 1], xmld[b][:],
            OP.mult, OP.add), rd=[b_yl[b], b_xl[b]], rdp=[b_tsc], wr=[b_os[b]])
        prog.dma("sync", out_t.ap()[nt * 128:(nt + 1) * 128, :], outst[b][:], rd=[b_os[b]], wr=[b_out])

    prog.barrier(prog.dsems + list(prog.csem.values()))
    esD.close()
    esC2.close()
    esF8.close()
    es_root.close()
    prog.emit()
    return nc, dbg_t



def _reorder_cols(w):
    J = w.shape[1]
    idx = np.arange(128).reshape(64, 2).T.reshape(-1)
    cols = np.concatenate([b * 128 + idx for b in range(J // 128)])
    return np.ascontiguousarray(w[:, cols])


def _tables(S):
    pos = np.arange(S, dtype=np.float32)
    i = np.arange(128, dtype=np.float32)
    theta = (1.0 / 10000.0 ** (2 * i / 256.0)).astype(np.float32)
    ang = pos[:, None] * theta[None, :]
    cos = np.cos(ang).astype(np.float32)
    sin = np.sin(ang).astype(np.float32)
    cosd = np.concatenate([cos[:, 0:64], cos[:, 64:128]] * 2, axis=1)
    sind = np.concatenate([sin[:, 0:64], sin[:, 64:128]] * 2, axis=1)
    return (np.ascontiguousarray(cosd).astype(ml_dtypes.bfloat16),
            np.ascontiguousarray(sind).astype(ml_dtypes.bfloat16))


def make_inputs(core_x, weights, S):
    cosd, sind = _tables(S)
    tri = np.tril(np.ones((128, 128), np.float32)).astype(ml_dtypes.bfloat16)
    m = {
        "x": np.ascontiguousarray(core_x),
        "wq": weights["wq_r"], "wk": weights["wk_r"], "wv": weights["wv"],
        "wo": weights["wo"], "wg": weights["wg"], "wu": weights["wu"],
        "wd": weights["wd"], "idx": weights["idx"],
        "lq1": weights["lq1"], "lq2": weights["lq2"],
        "lk1": weights["lk1"], "lk2": weights["lk2"],
        "rmsw": weights["rmsw"],
        "costab": cosd, "sintab": sind, "tri": tri,
        "identq": np.eye(128, dtype=np.float32).astype(ml_dtypes.float8_e4m3),
        "identf": np.eye(128, dtype=np.float32),
    }
    return m


def prep_weights(L0_attn_Wq, L0_attn_Wk, L0_attn_Wv, L0_attn_Wo,
                 L0_attn_lq1, L0_attn_lq2, L0_attn_lk1, L0_attn_lk2,
                 L0_attn_rms_w, L0_ffn_w_gate, L0_ffn_w_up, L0_ffn_w_down,
                 L0_ffn_down_reorder_index):
    f = np.asarray
    return {
        "wq_r": _reorder_cols(f(L0_attn_Wq, np.float32)),
        "wk_r": _reorder_cols(f(L0_attn_Wk, np.float32)),
        "wv": np.ascontiguousarray(f(L0_attn_Wv, np.float32)),
        "wo": np.ascontiguousarray(f(L0_attn_Wo, np.float32)),
        "wg": np.ascontiguousarray(f(L0_ffn_w_gate, np.float32)),
        "wu": np.ascontiguousarray(f(L0_ffn_w_up, np.float32)),
        "wd": np.ascontiguousarray(f(L0_ffn_w_down, np.float32)),
        "idx": np.ascontiguousarray(
            f(L0_ffn_down_reorder_index).astype(np.int32).reshape(IFF, 1)),
        "lq1": np.ascontiguousarray(f(L0_attn_lq1, np.float32)),
        "lq2": np.ascontiguousarray(f(L0_attn_lq2, np.float32)),
        "lk1": np.ascontiguousarray(f(L0_attn_lk1, np.float32)),
        "lk2": np.ascontiguousarray(f(L0_attn_lk2, np.float32)),
        "rmsw": np.ascontiguousarray(f(L0_attn_rms_w, np.float32).reshape(D, 1)),
    }


def kernel(L0_attn_x, L0_attn_Wq, L0_attn_Wk, L0_attn_Wv, L0_attn_Wo,
           L0_attn_lq1, L0_attn_lq2, L0_attn_lk1, L0_attn_lk2, L0_attn_rms_w,
           L0_ffn_w_gate, L0_ffn_w_up, L0_ffn_w_down, L0_ffn_up_reorder_index,
           L0_ffn_down_reorder_index):
    x = np.asarray(L0_attn_x, dtype=np.float32)
    BATCH, S, _ = x.shape
    weights = prep_weights(L0_attn_Wq, L0_attn_Wk, L0_attn_Wv, L0_attn_Wo,
                           L0_attn_lq1, L0_attn_lq2, L0_attn_lk1, L0_attn_lk2,
                           L0_attn_rms_w, L0_ffn_w_gate, L0_ffn_w_up,
                           L0_ffn_w_down, L0_ffn_down_reorder_index)
    nc, _ = build_nc(S=S)
    in_maps = [make_inputs(x[i], weights, S) for i in range(BATCH)]
    res = run_bass_kernel_spmd(nc, in_maps, core_ids=list(range(BATCH)))
    out = np.stack([np.asarray(r["out"], dtype=np.float32)
                    for r in res.results], axis=0)
    return out


# version 35
# speedup vs baseline: 1.2041x; 1.2041x over previous
import math
from contextlib import ExitStack

import numpy as np
import ml_dtypes
import concourse.bass as bass
import concourse.mybir as mybir
from concourse.bass_utils import run_bass_kernel_spmd

F32 = mybir.dt.float32
BF16 = mybir.dt.bfloat16
FP8 = mybir.dt.float8e4
I32 = mybir.dt.int32
AF = mybir.ActivationFunctionType
OP = mybir.AluOpType
AX = mybir.AxisListType

E = 2048
H = 16
KVH = 4
D = 128
IFF = 5632
SQD = 1.0 / math.sqrt(D)
LAM_INIT = 0.2
OUT_GAIN = 0.8


class Sem:
    __slots__ = ("h", "n", "eng")

    def __init__(self, h, eng=None):
        self.h = h
        self.n = 0
        self.eng = eng


class Buf:
    __slots__ = ("w", "rs", "dsem")

    def __init__(self):
        self.w = None
        self.rs = []
        self.dsem = None


class Prog:
    ENGS = ("sync", "tensor", "vector", "scalar", "gpsimd")

    def __init__(self, nc, es):
        self.nc = nc
        self.es = es
        self.items = {e: [] for e in self.ENGS}
        self._hw = {e: {} for e in self.ENGS}
        self._nds = 0
        self.dsems = []
        self.csem = {e: Sem(es.enter_context(nc.semaphore("c_" + e)), e)
                     for e in ("tensor", "vector", "scalar", "gpsimd", "sync")}
        self._tr_last = None

    def buf_dsem(self, b):
        if b.dsem is None:
            self._nds += 1
            b.dsem = Sem(self.es.enter_context(
                self.nc.semaphore(f"d{self._nds}")), None)
            self.dsems.append(b.dsem)
        return b.dsem

    def _waits(self, eng, toks, never_skip=False):
        out = []
        hw = self._hw[eng]
        for t in toks:
            if t is None:
                continue
            sem, val = t
            if val <= 0:
                continue
            if hw.get(id(sem), 0) >= val:
                continue
            hw[id(sem)] = val
            out.append((sem, val))
        return out

    def op(self, eng, fn, rd=(), rdp=(), wr=(), extra=(), sem=None, inck=1):
        toks = []
        for b in rd:
            toks.append(b.w)
        for b in wr:
            toks.append(b.w)
            toks.extend(b.rs)
        ptoks = [b.w for b in rdp]
        ws = self._waits(eng, toks + list(extra)) + \
            self._waits(eng, ptoks, never_skip=True)
        s = sem if sem is not None else self.csem[eng]
        s.n += inck
        tok = (s, s.n)
        self.items[eng].append((ws, fn, s, inck))
        for b in rd:
            b.rs.append(tok)
        for b in rdp:
            b.rs.append(tok)
        for b in wr:
            b.w = tok
            b.rs = []
        return tok

    def dma(self, eng, dst, src, rd=(), wr=(), extra=(), **kw):
        sem = self.buf_dsem(wr[0])
        return self.op(eng, lambda e: e.dma_start(dst, src, **kw),
                       rd=rd, wr=wr, extra=extra, sem=sem, inck=16)

    def trans(self, dst, src, rd=(), wr=(), extra=()):
        sem = self.buf_dsem(wr[0])
        tok = self.op("sync", lambda e: e.dma_start_transpose(dst, src),
                      rd=rd, wr=wr,
                      extra=list(extra) + ([self._tr_last]
                                           if self._tr_last else []),
                      sem=sem, inck=16)
        self._tr_last = tok
        return tok

    def barrier(self, sems):
        snap = [(s, s.n) for s in sems if s.n > 0]
        for e in self.ENGS:
            hw = self._hw[e]
            ws = []
            for (s, v) in snap:
                if s.eng == e or hw.get(id(s), 0) >= v:
                    continue
                hw[id(s)] = v
                ws.append((s, v))
            if ws:
                self.items[e].append((ws, None, None, 0))

    def emit(self):
        nc = self.nc
        with nc.Block() as block:
            def mk(name):
                def body(eng):
                    for (ws, fn, sem, inck) in self.items[name]:
                        for (s, v) in ws:
                            eng.wait_ge(s.h, v)
                        if fn is None:
                            continue
                        ins = fn(eng)
                        if sem is not None:
                            ins.then_inc(sem.h, inck)
                return body
            for e in self.ENGS:
                getattr(block, e)(mk(e))

    def n_instr(self):
        return {e: (len(v), sum(len(w[0]) for w in v)) for e, v in self.items.items()}


def build_nc(S=2048, phases="ABCD", dbg=(), KCH=1024):
    NT = S // 128
    nc = bass.Bass("TRN2", target_bir_lowering=False)

    din = lambda name, shape, dt: nc.dram_tensor(name, list(shape), dt,
                                                 kind="ExternalInput")
    x_in = din("x", [S, E], F32)
    wq = din("wq", [E, 2 * H * D], F32)
    wk = din("wk", [E, 2 * KVH * D], F32)
    wv = din("wv", [E, KVH * D], F32)
    wo_t = din("wo", [E, E], F32)
    wg_t = din("wg", [IFF, E], F32)
    wu_t = din("wu", [IFF, E], F32)
    wd_t = din("wd", [E, IFF], F32)
    idx_t = din("idx", [IFF, 1], I32)
    lq1 = din("lq1", [H, D], F32)
    lq2 = din("lq2", [H, D], F32)
    lk1 = din("lk1", [H, D], F32)
    lk2 = din("lk2", [H, D], F32)
    rmsw = din("rmsw", [D, 1], F32)
    costab = din("costab", [S, 256], F32)
    sintab = din("sintab", [S, 256], F32)
    tri_t = din("tri", [128, 128], BF16)
    identq_t = din("identq", [128, 128], FP8)
    identf_t = din("identf", [128, 128], F32)

    qs = nc.dram_tensor("qs", [NT * 4096, 128], BF16)
    xm = nc.dram_tensor("xm", [S, E], F32)
    tsc = nc.dram_tensor("tsc", [S, IFF], BF16)
    ysc = nc.dram_tensor("ysc", [S, E], BF16)
    out_t = nc.dram_tensor("out", [S, E], F32, kind="ExternalOutput")

    dbg_t = {}
    def dbg_out(name, shape, dt):
        if name in dbg:
            dbg_t[name] = nc.dram_tensor("dbg_" + name, list(shape), dt,
                                         kind="ExternalOutput")
        return dbg_t.get(name)

    dbg_out("lnxT", [128, 16 * S], BF16)
    dbg_out("KT", [128, 8 * S], BF16)
    dbg_out("V", [128, KVH * NT * 132], BF16)
    dbg_out("qs", [NT * 4096, 128], BF16)
    dbg_out("xmid", [S, E], F32)
    dbg_out("x0dbg", [128, S], BF16)
    dbg_out("qxT", [128, 16 * S], FP8)
    dbg_out("tsc", [S, IFF], BF16)
    dbg_out("qtT", [128, 44 * S], FP8)

    es_root = ExitStack()
    sbA = lambda n, sh, dt: es_root.enter_context(nc.sbuf_tensor(n, list(sh), dt))
    prog = Prog(nc, es_root)
    c_t = prog.csem["tensor"]
    c_v = prog.csem["vector"]
    c_s = prog.csem["scalar"]
    c_g = prog.csem["gpsimd"]

    B = lambda: Buf()
    bufs = []
    def NB(n=1):
        r = [Buf() for _ in range(n)]
        bufs.extend(r)
        return r if n > 1 else r[0]

    trib = sbA("trib", [128, 128], BF16); b_tri = NB()
    idq = sbA("idq", [128, 128], FP8); b_idq = NB()
    idf = sbA("idf", [128, 128], F32); b_idf = NB()
    rmsb = sbA("rmsb", [128, 1], F32); b_rms = NB()
    lamb = sbA("lamb", [128, 16], F32); b_lam = NB()
    idxb = sbA("idxb", [128, 44], I32); b_idx = NB()
    sc = {k: sbA("sc_" + k, [128, max(NT, 16)], F32) for k in
          ("t1", "t2", "m", "r", "nm", "sx")}
    b_sc = {k: NB() for k in sc}
    eps5 = sbA("eps5", [128, 1], F32); b_e5 = NB()
    eps8 = sbA("eps8", [128, 1], F32); b_e8 = NB()
    wdmax = sbA("wdmax", [128, 16], F32); b_wdm = NB()
    wdsc = sbA("wdsc", [128, 16], F32); b_wds = NB()
    tmaxb = sbA("tmaxb", [128, NT], F32); b_tmx = NB()
    tscb = sbA("tscb", [128, NT], F32); b_tsc = NB()
    qscb = sbA("qscb", [128, NT], F32); b_qsc = NB()

    b_qs, b_xm, b_ts, b_ys, b_out = NB(), NB(), NB(), NB(), NB()
    b_x = NB()
    prog.dma("sync", trib[:], tri_t.ap(), wr=[b_tri])
    prog.op("vector", lambda e: e.memset(eps5[:], 1e-5), wr=[b_e5])
    prog.op("vector", lambda e: e.memset(eps8[:], 1e-8), wr=[b_e8])
    prog.dma("sync", idq[:], identq_t.ap(), wr=[b_idq])
    prog.dma("sync", idf[:], identf_t.ap(), wr=[b_idf])
    prog.dma("sync", rmsb[:], rmsw.ap(), wr=[b_rms])
    prog.dma("sync", idxb[:], idx_t.ap().rearrange("(c p) o -> p (c o)", p=128),
             wr=[b_idx], allow_slow_non_contiguous=True)

    lqb = sbA("lqb", [128, 512], F32); b_lq = NB()
    for i, t in enumerate([lq1, lq2, lk1, lk2]):
        prog.dma("sync", lqb[0:16, i * 128:(i + 1) * 128], t.ap(), wr=[b_lq])
    lsc1 = sbA("lsc1", [128, 128], F32); b_l1 = NB()
    lsc2 = sbA("lsc2", [16, 2], F32); b_l2 = NB()
    lsc3 = sbA("lsc3", [16, 2], F32); b_l3 = NB()
    lam16 = sbA("lam16", [16, 1], F32); b_l4 = NB()
    lrow = sbA("lrow", [1, 16], F32); b_l5 = NB()
    onesc = sbA("onesc", [1, 128], F32); b_l6 = NB()

    esAB = ExitStack()
    KT = esAB.enter_context(nc.sbuf_tensor("KT", [128, 8 * S], BF16)); b_KT = NB()
    Vx = esAB.enter_context(nc.sbuf_tensor("Vx", [128, KVH * NT * 132], BF16))
    b_V = NB()
    esA = ExitStack()
    sbP = lambda n, sh, dt: esA.enter_context(nc.sbuf_tensor(n, list(sh), dt))
    cosb = sbP("cosb", [128, NT * 256], F32); b_cos = NB()
    sinb = sbP("sinb", [128, NT * 256], F32); b_sin = NB()
    lnxT = sbP("lnxT", [128, 16 * S], BF16); b_lnT = NB(NT) if NT > 1 else [NB()]
    xst = [sbP(f"xst{i}", [128, E], F32) for i in range(2)]; b_xst = NB(2)
    sqscr = sbP("sqscr", [128, E], F32); b_sq = NB()
    lnst = [sbP(f"lnst{i}", [128, E], BF16) for i in range(2)]; b_lns = NB(2)
    wst = [sbP(f"wst{i}", [128, 8 * 512], BF16) for i in range(2)]; b_wst = NB(2)
    qro = [sbP(f"qro{i}", [128, 512], BF16) for i in range(2)]; b_qro = NB(2)
    qtr = [sbP(f"qtr{i}", [128, 512], BF16) for i in range(2)]; b_qtr = NB(2)
    pA = [esA.enter_context(nc.psum_tensor(f"pA{i}", [128, 512], F32))
          for i in range(8)]
    b_pA = NB(8)

    prog.op("vector", lambda e: e.tensor_mul(lsc1[0:16, :], lqb[0:16, 0:128],
                                             lqb[0:16, 256:384]),
            rd=[b_lq], wr=[b_l1])
    prog.op("vector", lambda e: e.reduce_sum(lsc2[:, 0:1], lsc1[0:16, :], AX.X),
            rd=[b_l1], wr=[b_l2])
    prog.op("vector", lambda e: e.tensor_mul(lsc1[0:16, :], lqb[0:16, 128:256],
                                             lqb[0:16, 384:512]),
            rd=[b_lq], wr=[b_l1])
    prog.op("vector", lambda e: e.reduce_sum(lsc2[:, 1:2], lsc1[0:16, :], AX.X),
            rd=[b_l1], wr=[b_l2])
    prog.op("scalar", lambda e: e.activation(lsc3[:, :], lsc2[:, :], AF.Exp),
            rd=[b_l2], wr=[b_l3])
    prog.op("vector", lambda e: e.tensor_sub(lam16[:, :], lsc3[:, 0:1], lsc3[:, 1:2]),
            rd=[b_l3], wr=[b_l4])
    prog.op("vector", lambda e: e.tensor_scalar_add(lam16[:, :], lam16[:, :], LAM_INIT),
            rd=[b_l4], wr=[b_l4])
    prog.op("tensor", lambda e: e.transpose(pA[0][0:1, 0:16], lam16[:, :],
                                            idf[0:16, 0:16]),
            rd=[b_l4, b_idf], wr=[b_pA[0]])
    prog.op("vector", lambda e: e.tensor_copy(lrow[:, :], pA[0][0:1, 0:16]),
            rd=[b_pA[0]], wr=[b_l5])
    prog.op("vector", lambda e: e.memset(onesc[:], 1.0), wr=[b_l6])
    prog.op("tensor", lambda e: e.matmul(pA[0][:, 0:16], onesc[:], lrow[:, :],
                                         start=True, stop=True),
            rd=[b_l5, b_l6], wr=[b_pA[0]])
    prog.op("vector", lambda e: e.tensor_copy(lamb[:], pA[0][:, 0:16]),
            rd=[b_pA[0]], wr=[b_lam])

    prog.dma("sync", cosb[:].rearrange("p (t f) -> p t f", f=256),
             costab.ap().rearrange("(t p) f -> p t f", p=128), wr=[b_cos])
    prog.dma("sync", sinb[:].rearrange("p (t f) -> p t f", f=256),
             sintab.ap().rearrange("(t p) f -> p t f", p=128), wr=[b_sin])
    prog.op("vector", lambda e: e.memset(
        Vx[:].rearrange("p (s f) -> p s f", f=132)[:, :, 128:132], 1.0), wr=[b_V])

    for nt in range(NT):
        b = nt % 2
        prog.dma("sync", xst[b][:], x_in.ap()[nt * 128:(nt + 1) * 128, :],
                 rd=[b_x], wr=[b_xst[b]])
        prog.op("vector", lambda e, b=b, nt=nt: e.reduce_sum(
            sc["t1"][:, nt:nt + 1], xst[b][:], AX.X),
            rd=[b_xst[b]], wr=[b_sc["t1"]])
        prog.op("scalar", lambda e, b=b, nt=nt: e.activation(
            sqscr[:], xst[b][:], AF.Square, accum_out=sc["t2"][:, nt:nt + 1]),
            rd=[b_xst[b]], wr=[b_sq, b_sc["t2"]])
        prog.op("vector", lambda e, nt=nt: e.tensor_scalar_mul(
            sc["m"][:, nt:nt + 1], sc["t1"][:, nt:nt + 1], 1.0 / E),
            rd=[b_sc["t1"]], wr=[b_sc["m"]])
        prog.op("vector", lambda e, nt=nt: e.tensor_mul(
            sc["r"][:, nt:nt + 1], sc["m"][:, nt:nt + 1], sc["m"][:, nt:nt + 1]),
            rd=[b_sc["m"]], wr=[b_sc["r"]])
        prog.op("vector", lambda e, nt=nt: e.scalar_tensor_tensor(
            sc["r"][:, nt:nt + 1], sc["t2"][:, nt:nt + 1], 1.0 / E,
            sc["r"][:, nt:nt + 1], OP.mult, OP.subtract),
            rd=[b_sc["t2"], b_sc["r"]], wr=[b_sc["r"]])
        prog.op("scalar", lambda e, nt=nt: e.activation(
            sc["r"][:, nt:nt + 1], sc["r"][:, nt:nt + 1], AF.Sqrt, bias=eps5[:]),
            rd=[b_sc["r"]], rdp=[b_e5], wr=[b_sc["r"]])
        prog.op("vector", lambda e, nt=nt: e.reciprocal(
            sc["r"][:, nt:nt + 1], sc["r"][:, nt:nt + 1]),
            rd=[b_sc["r"]], wr=[b_sc["r"]])
        prog.op("vector", lambda e, nt=nt: e.scalar_tensor_tensor(
            sc["nm"][:, nt:nt + 1], sc["m"][:, nt:nt + 1], -1.0,
            sc["r"][:, nt:nt + 1], OP.mult, OP.mult),
            rd=[b_sc["m"], b_sc["r"]], wr=[b_sc["nm"]])
        prog.op("scalar", lambda e, b=b, nt=nt: e.activation(
            lnst[b][:], xst[b][:], AF.Identity, bias=sc["nm"][:, nt:nt + 1],
            scale=sc["r"][:, nt:nt + 1]),
            rd=[b_xst[b]], rdp=[b_sc["nm"], b_sc["r"]], wr=[b_lns[b]])
        prog.trans(lnxT[:].rearrange("p (ct n) -> p ct n", n=S)
                   [:, :, nt * 128:(nt + 1) * 128],
                   lnst[b][:].rearrange("p (ct f) -> p ct f", f=128),
                   rd=[b_lns[b]], wr=[b_lnT[nt]])

    jchunks = [("q", j, wq) for j in range(8)] + \
              [("k", j, wk) for j in range(2)] + [("v", 0, wv)]
    groups = [list(range(0, min(8, NT)))]
    if NT > 8:
        groups.append(list(range(8, NT)))

    for grp in groups:
        for jci, (mat, jc, wsrc) in enumerate(jchunks):
            for half8 in range(2):
                bw = (2 * jci + half8) % 2
                prog.dma("gpsimd",
                         wst[bw][:].rearrange("p (t f) -> p t f", f=512),
                         wsrc.ap()[:, jc * 512:(jc + 1) * 512].rearrange(
                             "(t p) f -> p t f", p=128)[:, half8 * 8:half8 * 8 + 8, :],
                         wr=[b_wst[bw]])
                for ct8 in range(8):
                    ct = half8 * 8 + ct8
                    for i, nt in enumerate(grp):
                        prog.op("tensor", lambda e, ct=ct, ct8=ct8, nt=nt, i=i,
                                bw=bw: e.matmul(
                            pA[i][:],
                            lnxT[:, ct * S + nt * 128: ct * S + nt * 128 + 128],
                            wst[bw][:, ct8 * 512:(ct8 + 1) * 512],
                            start=(ct == 0), stop=(ct == 15)),
                            rd=[b_lnT[nt], b_wst[bw]],
                            wr=[b_pA[i]] if ct in (0, 15) else ())
            for i, nt in enumerate(grp):
                bq = i % 2
                if mat == "v":
                    dstv = Vx[:].rearrange("p (s f) -> p s f", f=132) \
                        .rearrange("p (g t) f -> p g t f", g=KVH)[:, :, nt, 0:128]
                    prog.op("scalar", lambda e, i=i, dstv=dstv: e.activation(
                        dstv, pA[i][:].rearrange("p (g f) -> p g f", g=4), AF.Copy),
                        rd=[b_pA[i]], wr=[b_V])
                    continue
                tb_c = cosb[:, nt * 256:(nt + 1) * 256].rearrange("p (b f) -> p b f", b=4)
                tb_s = sinb[:, nt * 256:(nt + 1) * 256].rearrange("p (b f) -> p b f", b=4)
                src = pA[i][:].rearrange("p (b h f) -> p b h f", b=4, h=2)
                dstr = qro[bq][:].rearrange("p (b h f) -> p b h f", b=4, h=2)
                tmp = qtr[bq][:].rearrange("p (b h f) -> p b h f", b=4, h=2)
                re_, im_ = src[:, :, 0, :], src[:, :, 1, :]
                ore, oim = dstr[:, :, 0, :], dstr[:, :, 1, :]
                t1a, t1b = tmp[:, :, 0, :], tmp[:, :, 1, :]
                prog.op("vector", lambda e, a=t1a, x=im_, t=tb_s: e.tensor_mul(a, x, t),
                        rd=[b_pA[i], b_sin], wr=[b_qtr[bq]])
                prog.op("vector", lambda e, a=t1b, x=re_, t=tb_s: e.tensor_mul(a, x, t),
                        rd=[b_pA[i], b_sin], wr=[b_qtr[bq]])
                prog.op("vector", lambda e, a=ore, x=re_, t=tb_c: e.tensor_mul(a, x, t),
                        rd=[b_pA[i], b_cos], wr=[b_qro[bq]])
                prog.op("vector", lambda e, a=oim, x=im_, t=tb_c: e.tensor_mul(a, x, t),
                        rd=[b_pA[i], b_cos], wr=[b_qro[bq]])
                prog.op("vector", lambda e, a=ore, bb=t1a: e.tensor_sub(a, a, bb),
                        rd=[b_qtr[bq], b_qro[bq]], wr=[b_qro[bq]])
                prog.op("vector", lambda e, a=oim, bb=t1b: e.tensor_add(a, a, bb),
                        rd=[b_qtr[bq], b_qro[bq]], wr=[b_qro[bq]])
                if mat == "q":
                    prog.trans(qtr[bq][:].rearrange("p (t f) -> p t f", f=128),
                               qro[bq][:].rearrange("p (t f) -> p t f", f=128),
                               rd=[b_qro[bq]], wr=[b_qtr[bq]])
                    off = (nt * 8 + jc) * 512
                    prog.dma("sync",
                             qs.ap()[off:off + 512, :].rearrange(
                                 "(t p) f -> p t f", p=128),
                             qtr[bq][:].rearrange("p (t f) -> p t f", f=128),
                             rd=[b_qtr[bq]], wr=[b_qs])
                else:
                    dstk = KT[:].rearrange("p (bb n) -> p bb n", n=S)[
                        :, 4 * jc:4 * jc + 4, nt * 128:(nt + 1) * 128]
                    prog.trans(dstk, qro[bq][:].rearrange("p (t f) -> p t f", f=128),
                               rd=[b_qro[bq]], wr=[b_KT])

    if "lnxT" in dbg_t:
        prog.dma("sync", dbg_t["lnxT"].ap(), lnxT[:, :], rd=[b_lnT], wr=[NB()])
    if "KT" in dbg_t:
        prog.dma("sync", dbg_t["KT"].ap(), KT[:, :], rd=[b_KT], wr=[NB()])
    if "V" in dbg_t:
        prog.dma("sync", dbg_t["V"].ap(), Vx[:, :], rd=[b_V], wr=[NB()])
    if "qs" in dbg_t:
        prog.dma("sync", dbg_t["qs"].ap(), qs.ap(), rd=[b_qs], wr=[NB()])

    if "B" not in phases:
        prog.barrier(prog.dsems + list(prog.csem.values()))
        esA.close()
        esAB.close()
        es_root.close()
        prog.emit()
        return nc, dbg_t

    prog.barrier(prog.dsems + list(prog.csem.values()))
    esA.close()

    esB = ExitStack()
    sbB = lambda n, sh, dt: esB.enter_context(nc.sbuf_tensor(n, list(sh), dt))
    wob = sbB("wob", [128, 16 * E], BF16); b_wo = NB()
    qT = [sbB(f"qT{i}", [128, 4096], BF16) for i in range(2)]; b_qT = NB(2)
    x0b = [sbB(f"x0b{i}", [128, S], BF16) for i in range(2)]; b_x0 = NB(2)
    x1b = [sbB(f"x1b{i}", [128, S], BF16) for i in range(2)]; b_x1 = NB(2)
    exb = [sbB(f"exb{i}", [128, S], BF16) for i in range(3)]; b_ex = NB(3)
    exT = [sbB(f"exT{i}", [128, 16 * 128], BF16) for i in range(3)]; b_exT = NB(3)
    attr = sbB("attr", [128, H * 128], BF16); b_ar = NB()
    attT = sbB("attT", [128, H * 128], BF16); b_aT = NB()
    xinb = [sbB(f"xinb{i}", [128, E], F32) for i in range(2)]
    b_xin = NB(2)
    xmidb = sbB("xmidb", [128, E], F32); b_xmb = NB()
    sqs2 = [sbB(f"sqs2_{i}", [128, 128], BF16) for i in range(2)]; b_sq2 = NB(2)
    sqsq = sbB("sqsq", [128, 128], BF16); b_sqq = NB()
    z0p = [sbB(f"z0p{i}", [128, 8], F32) for i in range(2)]; b_z0p = NB(2)
    z1p = [sbB(f"z1p{i}", [128, 8], F32) for i in range(2)]; b_z1p = NB(2)
    zsc = [sbB(f"zsc{i}", [128, 6], F32) for i in range(2)]; b_zsc = NB(2)
    m2b = sbB("m2b", [128, 16], F32); b_m2 = NB()
    zdb = sbB("zdb", [128, 16], F32); b_zd = NB()
    stot = sbB("stot", [128, 16], F32); b_st = NB()
    st1 = sbB("st1", [128, 16], F32); b_st1 = NB()
    pE = [esB.enter_context(nc.psum_tensor(f"pE{i}", [128, 1024], F32))
          for i in range(2)]
    b_pE = NB(2)
    pATT = [esB.enter_context(nc.psum_tensor(f"pATT{i}", [128, 512], F32))
            for i in range(2)]
    b_pAT = NB(2)
    pWO = [esB.enter_context(nc.psum_tensor(f"pWO{i}", [128, 512], F32))
           for i in range(2)]
    b_pW = NB(2)

    for et in range(16):
        prog.dma("gpsimd", wob[:, et * E:(et + 1) * E],
                 wo_t.ap()[et * 128:(et + 1) * 128, :], wr=[b_wo])
        prog.op("vector", lambda e, et=et: e.tensor_scalar_mul(
            wob[:, et * E:(et + 1) * E], wob[:, et * E:(et + 1) * E], rmsb[:]),
            rd=[b_wo], rdp=[b_rms], wr=[b_wo])

    pend_wo = [None]
    for nt in range(NT):
        bq = nt % 2
        prog.dma("sync", qT[bq][:].rearrange("p (t f) -> p t f", f=128),
                 qs.ap()[nt * 4096:(nt + 1) * 4096, :].rearrange(
                     "(t p) f -> p t f", p=128), rd=[b_qs], wr=[b_qT[bq]])
        prog.dma("sync", xinb[bq][:], x_in.ap()[nt * 128:(nt + 1) * 128, :],
                 rd=[b_x], wr=[b_xin[bq]])
        def stage1(h):
            g = h // 4
            hp = h % 2
            b_zs = b_zsc[hp]
            zs = zsc[hp]
            kch = [(k0, min(KCH, nt * 128 - k0))
                   for k0 in range(0, nt * 128, KCH)]
            nci = len(kch)
            cfg = []
            for (a, xout, b_xoL, zp, b_zpL) in (
                    (0, x0b, b_x0, z0p, b_z0p), (1, x1b, b_x1, z1p, b_z1p)):
                cfg.append((a, xout[hp], b_xoL[hp], zp[hp], b_zpL[hp],
                            qT[bq][:, (2 * h + a) * 128:(2 * h + a) * 128 + 128],
                            KT[:].rearrange("p (bb n) -> p bb n", n=S)[:, 2 * g + a, :],
                            pE[a], b_pE[a]))
            for ci, (k0, kw) in enumerate(kch):
                for (a, xout, b_xo, zp, b_zp, lq_ap, kt_ap, pe, b_pe) in cfg:
                    for s0 in range(0, kw, 512):
                        sw = min(512, kw - s0)
                        prog.op("tensor", lambda e, pe=pe, lq=lq_ap, kt=kt_ap,
                                k0=k0, s0=s0, sw=sw: e.matmul(
                                    pe[:, s0:s0 + sw], lq,
                                    kt[:, k0 + s0:k0 + s0 + sw],
                                    start=True, stop=True),
                                rd=[b_qT[bq], b_KT],
                                wr=[b_pe] if (s0 == 0 or s0 + sw == kw) else ())
                    prog.op("scalar", lambda e, pe=pe, xo=xout, k0=k0, kw=kw, zp=zp,
                            ci=ci: e.activation(
                                xo[:, k0:k0 + kw], pe[:, 0:kw], AF.Exp, scale=SQD,
                                accum_out=zp[:, ci:ci + 1]),
                            rd=[b_pe], wr=[b_xo, b_zp])
            for (a, xout, b_xo, zp, b_zp, lq_ap, kt_ap, pe, b_pe) in cfg:
                prog.op("tensor", lambda e, lq=lq_ap, kt=kt_ap, nt=nt, pe=pe:
                        e.matmul(
                    pe[:, 512:640], lq, kt[:, nt * 128:nt * 128 + 128],
                    start=True, stop=True), rd=[b_qT[bq], b_KT], wr=[b_pe])
                prog.op("scalar", lambda e, hp=hp, pe=pe: e.activation(
                    sqs2[hp][:], pe[:, 512:640], AF.Exp, scale=SQD),
                    rd=[b_pe], wr=[b_sq2[hp]])
                prog.op("vector", lambda e, xo=xout, nt=nt, zp=zp, nci=nci, hp=hp:
                        e.scalar_tensor_tensor(
                            xo[:, nt * 128:(nt + 1) * 128], sqs2[hp][:], 1.0, trib[:],
                            OP.mult, OP.mult, accum_out=zp[:, nci:nci + 1]),
                        rd=[b_sq2[hp], b_tri], wr=[b_xo, b_zp])


        def stage1b(h):
            g = h // 4
            hp = h % 2
            he = h % 3
            b_zs = b_zsc[hp]
            zs = zsc[hp]
            kch = [(k0, min(KCH, nt * 128 - k0))
                   for k0 in range(0, nt * 128, KCH)]
            ncht = len(kch) + 1
            prog.op("vector", lambda e, n=ncht, zs=zs, hp=hp: e.reduce_sum(
                zs[:, 0:1], z0p[hp][:, 0:n], AX.X), rd=[b_z0p[hp]], wr=[b_zs])
            prog.op("vector", lambda e, n=ncht, zs=zs, hp=hp: e.reduce_sum(
                zs[:, 1:2], z1p[hp][:, 0:n], AX.X), rd=[b_z1p[hp]], wr=[b_zs])
            prog.op("vector", lambda e, zs=zs: e.reciprocal(zs[:, 2:3], zs[:, 0:1]),
                    rd=[b_zs], wr=[b_zs])
            prog.op("vector", lambda e, zs=zs: e.reciprocal(zs[:, 3:4], zs[:, 1:2]),
                    rd=[b_zs], wr=[b_zs])
            prog.op("vector", lambda e, zs=zs: e.tensor_mul(zs[:, 3:4], zs[:, 3:4],
                                                            zs[:, 0:1]),
                    rd=[b_zs], wr=[b_zs])
            prog.op("vector", lambda e, h=h, zs=zs: e.tensor_scalar(
                zs[:, 3:4], zs[:, 3:4], lamb[:, h:h + 1], -1.0, OP.mult, OP.mult),
                rd=[b_zs], rdp=[b_lam], wr=[b_zs])
            kwa = (nt + 1) * 128
            prog.op("vector", lambda e, kw=kwa, hp=hp, zs=zs: e.scalar_tensor_tensor(
                x1b[hp][:, 0:kw], x1b[hp][:, 0:kw], zs[:, 3:4], x0b[hp][:, 0:kw],
                OP.mult, OP.add), rd=[b_x1[hp], b_x0[hp]], rdp=[b_zs],
                wr=[b_x1[hp]])
            he = h % 3
            prog.op("scalar", lambda e, kw=kwa, hp=hp, he=he, zs=zs: e.activation(
                exb[he][:, 0:kw], x1b[hp][:, 0:kw], AF.Exp, scale=zs[:, 2:3]),
                rd=[b_x1[hp]], rdp=[b_zs], wr=[b_ex[he]])
            prog.op("vector", lambda e, nt=nt, he=he: e.tensor_mul(
                exb[he][:, nt * 128:(nt + 1) * 128],
                exb[he][:, nt * 128:(nt + 1) * 128],
                trib[:]), rd=[b_ex[he], b_tri], wr=[b_ex[he]])
            if "x0dbg" in dbg_t and nt == 1 and h == 2:
                prog.dma("sync", dbg_t["x0dbg"].ap(), exb[he][:, 0:S],
                         rd=[b_ex[he]], wr=[NB()])
            prog.trans(exT[he][:].rearrange("p (t f) -> p t f", f=128)[:, 0:nt + 1, :],
                       exb[he][:, 0:kwa].rearrange("p (t f) -> p t f", f=128),
                       rd=[b_ex[he]], wr=[b_exT[he]])

        def stage2(h):
            g = h // 4
            he = h % 3
            pat, b_pat = pATT[h % 2], b_pAT[h % 2]
            for kt in range(nt + 1):
                vsl = Vx[:].rearrange("p (s f) -> p s f", f=132)[:, g * NT + kt, 0:129]
                prog.op("tensor", lambda e, kt=kt, vsl=vsl, pat=pat, nt=nt, he=he:
                        e.matmul(
                    pat[:, 0:129], exT[he][:, kt * 128:(kt + 1) * 128], vsl,
                    start=(kt == 0), stop=(kt == nt)),
                    rd=[b_exT[he], b_V], wr=[b_pat] if kt in (0, nt) else ())
            prog.op("scalar", lambda e, h=h, pat=pat: e.activation(
                sqsq[:], pat[:, 0:128], AF.Square, accum_out=m2b[:, h:h + 1]),
                rd=[b_pat], wr=[b_sqq, b_m2])
            prog.op("scalar", lambda e, h=h, pat=pat: e.activation(
                attr[:, h * 128:(h + 1) * 128], pat[:, 0:128], AF.Copy),
                rd=[b_pat], wr=[b_ar])
            prog.op("vector", lambda e, h=h, pat=pat: e.tensor_copy(
                zdb[:, h:h + 1], pat[:, 128:129]), rd=[b_pat], wr=[b_zd])

        for h in range(H + 1):
            if h < H:
                stage1(h)
                stage1b(h)
            if h >= 1:
                stage2(h - 1)
        def epilogue(nt=nt, bq=bq):
            prog.op("vector", lambda e: e.reciprocal(st1[:], zdb[:]),
                    rd=[b_zd], wr=[b_st1])
            prog.op("vector", lambda e: e.tensor_mul(m2b[:], m2b[:], st1[:]),
                    rd=[b_st1, b_m2], wr=[b_m2])
            prog.op("vector", lambda e: e.tensor_mul(m2b[:], m2b[:], st1[:]),
                    rd=[b_st1, b_m2], wr=[b_m2])
            prog.op("vector", lambda e: e.tensor_scalar_mul(m2b[:], m2b[:], 1.0 / D),
                    rd=[b_m2], wr=[b_m2])
            prog.op("scalar", lambda e: e.activation(m2b[:], m2b[:], AF.Sqrt, bias=eps8[:]),
                    rd=[b_m2], rdp=[b_e8], wr=[b_m2])
            prog.op("vector", lambda e: e.reciprocal(m2b[:], m2b[:]),
                    rd=[b_m2], wr=[b_m2])
            prog.op("vector", lambda e: e.scalar_tensor_tensor(
                stot[:], m2b[:], OUT_GAIN, st1[:], OP.mult, OP.mult),
                rd=[b_m2, b_st1], wr=[b_st])
            for h in range(H):
                prog.op("vector", lambda e, h=h: e.tensor_scalar_mul(
                    attr[:, h * 128:(h + 1) * 128], attr[:, h * 128:(h + 1) * 128],
                    stot[:, h:h + 1]), rd=[b_ar], rdp=[b_st], wr=[b_ar])
            prog.trans(attT[:].rearrange("p (t f) -> p t f", f=128),
                       attr[:].rearrange("p (t f) -> p t f", f=128),
                       rd=[b_ar], wr=[b_aT])


            def wo_part(nt=nt, bq=bq):
                for oc in range(4):
                    pw, b_pw = pWO[oc % 2], b_pW[oc % 2]
                    for et in range(16):
                        prog.op("tensor", lambda e, oc=oc, et=et, pw=pw: e.matmul(
                            pw[:], attT[:, et * 128:(et + 1) * 128],
                            wob[:, et * E + oc * 512: et * E + oc * 512 + 512],
                            start=(et == 0), stop=(et == 15)),
                            rd=[b_aT, b_wo], wr=[b_pw] if et in (0, 15) else ())
                    prog.op("vector", lambda e, oc=oc, pw=pw: e.scalar_tensor_tensor(
                        xmidb[:, oc * 512:(oc + 1) * 512], pw[:], 1.0,
                        xinb[bq][:, oc * 512:(oc + 1) * 512], OP.mult, OP.add),
                        rd=[b_pw, b_xin[bq]], wr=[b_xmb])
                prog.dma("sync", xm.ap()[nt * 128:(nt + 1) * 128, :], xmidb[:], rd=[b_xmb], wr=[b_xm])

            wo_part()
        epilogue()

    if pend_wo[0] is not None:
        pend_wo[0]()
        pend_wo[0] = None

    if "xmid" in dbg_t:
        prog.dma("sync", dbg_t["xmid"].ap(), xm.ap(), rd=[b_xm], wr=[NB()])

    if "C" not in phases:
        prog.barrier(prog.dsems + list(prog.csem.values()))
        esB.close()
        esAB.close()
        es_root.close()
        prog.emit()
        return nc, dbg_t

    prog.barrier(prog.dsems + list(prog.csem.values()))
    esB.close()
    esAB.close()

    esC = ExitStack()
    sbC = lambda n, sh, dt: esC.enter_context(nc.sbuf_tensor(n, list(sh), dt))
    qxT8 = sbC("qxT8", [128, 16 * S], FP8); b_qx = NB()
    xstC = [sbC(f"xstC{i}", [128, E], F32) for i in range(2)]; b_xsC = NB(2)
    sqsC = sbC("sqsC", [128, E], F32); b_sqC = NB()
    qx8s = sbC("qx8s", [128, E], FP8); b_q8 = NB()
    wgf = [sbC(f"wgf{i}", [128, E], BF16) for i in range(2)]; b_wgf = NB(2)
    wuf = [sbC(f"wuf{i}", [128, E], BF16) for i in range(2)]; b_wuf = NB(2)
    wdum = sbC("wdum", [128, E], BF16); b_wdum = NB()
    w8s = [sbC(f"w8s{i}", [128, E], FP8) for i in range(2)]; b_w8 = NB(2)
    wsc2 = [sbC(f"wsc2{i}", [128, E], BF16) for i in range(4)]; b_ws2 = NB(4)
    wgT = [sbC(f"wgT{i}", [128, 16 * 512], BF16) for i in range(3)]; b_wgT = NB(3)
    wuT = [sbC(f"wuT{i}", [128, 16 * 512], BF16) for i in range(3)]; b_wuT = NB(3)
    wm1 = sbC("wm1", [128, 16], F32)
    b_wm1 = NB(8)
    sgb = sbC("sgb", [128, 512], BF16); b_sg = NB()
    tst = sbC("tst", [128, 512], BF16); b_tst = NB()
    esF8 = ExitStack()
    pF8 = esF8.enter_context(nc.psum_tensor("pF8", [128, 8192], FP8))
    b_pF = NB()
    pF8v = pF8[:].rearrange("p (n two) -> p n two", two=2)
    pF8g = lambda n_t: pF8[:].rearrange(
        "p (it f two) -> p it f two", two=2, f=128)[:, 0:n_t, :, 0:1]
    pG = [esC.enter_context(nc.psum_tensor(f"pG{i}", [128, 512], F32))
          for i in range(4)]
    b_pG = NB(4)

    for nt in range(NT):
        b = nt % 2
        prog.dma("sync", xstC[b][:], xm.ap()[nt * 128:(nt + 1) * 128, :],
                 rd=[b_xm], wr=[b_xsC[b]])
        prog.op("vector", lambda e, b=b, nt=nt: e.reduce_sum(
            sc["t1"][:, nt:nt + 1], xstC[b][:], AX.X),
            rd=[b_xsC[b]], wr=[b_sc["t1"]])
        prog.op("scalar", lambda e, b=b, nt=nt: e.activation(
            sqsC[:], xstC[b][:], AF.Square, accum_out=sc["t2"][:, nt:nt + 1]),
            rd=[b_xsC[b]], wr=[b_sqC, b_sc["t2"]])
        prog.op("vector", lambda e, b=b, nt=nt: e.reduce_max(
            sc["m"][:, nt:nt + 1], xstC[b][:], AX.X),
            rd=[b_xsC[b]], wr=[b_sc["m"]])
        prog.op("vector", lambda e, b=b, nt=nt: e.tensor_reduce(
            sc["nm"][:, nt:nt + 1], xstC[b][:], AX.X, OP.min),
            rd=[b_xsC[b]], wr=[b_sc["nm"]])
        prog.op("vector", lambda e, nt=nt: e.tensor_scalar_mul(
            sc["t1"][:, nt:nt + 1], sc["t1"][:, nt:nt + 1], 1.0 / E),
            rd=[b_sc["t1"]], wr=[b_sc["t1"]])
        prog.op("vector", lambda e, nt=nt: e.tensor_mul(
            sc["r"][:, nt:nt + 1], sc["t1"][:, nt:nt + 1], sc["t1"][:, nt:nt + 1]),
            rd=[b_sc["t1"]], wr=[b_sc["r"]])
        prog.op("vector", lambda e, nt=nt: e.scalar_tensor_tensor(
            sc["r"][:, nt:nt + 1], sc["t2"][:, nt:nt + 1], 1.0 / E,
            sc["r"][:, nt:nt + 1], OP.mult, OP.subtract),
            rd=[b_sc["t2"], b_sc["r"]], wr=[b_sc["r"]])
        prog.op("scalar", lambda e, nt=nt: e.activation(
            sc["r"][:, nt:nt + 1], sc["r"][:, nt:nt + 1], AF.Sqrt, bias=eps5[:]),
            rd=[b_sc["r"]], rdp=[b_e5], wr=[b_sc["r"]])
        prog.op("vector", lambda e, nt=nt: e.reciprocal(
            sc["r"][:, nt:nt + 1], sc["r"][:, nt:nt + 1]),
            rd=[b_sc["r"]], wr=[b_sc["r"]])
        prog.op("vector", lambda e, nt=nt: e.tensor_sub(
            sc["m"][:, nt:nt + 1], sc["m"][:, nt:nt + 1], sc["t1"][:, nt:nt + 1]),
            rd=[b_sc["m"], b_sc["t1"]], wr=[b_sc["m"]])
        prog.op("vector", lambda e, nt=nt: e.tensor_sub(
            sc["nm"][:, nt:nt + 1], sc["t1"][:, nt:nt + 1], sc["nm"][:, nt:nt + 1]),
            rd=[b_sc["nm"], b_sc["t1"]], wr=[b_sc["nm"]])
        prog.op("vector", lambda e, nt=nt: e.tensor_max(
            sc["m"][:, nt:nt + 1], sc["m"][:, nt:nt + 1], sc["nm"][:, nt:nt + 1]),
            rd=[b_sc["m"], b_sc["nm"]], wr=[b_sc["m"]])
        prog.op("vector", lambda e, nt=nt: e.tensor_mul(
            sc["sx"][:, nt:nt + 1], sc["m"][:, nt:nt + 1], sc["r"][:, nt:nt + 1]),
            rd=[b_sc["m"], b_sc["r"]], wr=[b_sc["sx"]])
        prog.op("vector", lambda e, nt=nt: e.tensor_scalar_mul(
            qscb[:, nt:nt + 1], sc["sx"][:, nt:nt + 1], 1.0 / 224.0),
            rd=[b_sc["sx"]], wr=[b_qsc])
        prog.op("vector", lambda e, nt=nt: e.reciprocal(
            sc["t2"][:, nt:nt + 1], sc["sx"][:, nt:nt + 1]),
            rd=[b_sc["sx"]], wr=[b_sc["t2"]])
        prog.op("vector", lambda e, nt=nt: e.scalar_tensor_tensor(
            sc["t2"][:, nt:nt + 1], sc["t2"][:, nt:nt + 1], 224.0,
            sc["r"][:, nt:nt + 1], OP.mult, OP.mult),
            rd=[b_sc["r"], b_sc["t2"]], wr=[b_sc["t2"]])
        prog.op("vector", lambda e, nt=nt: e.scalar_tensor_tensor(
            sc["nm"][:, nt:nt + 1], sc["t1"][:, nt:nt + 1], -1.0,
            sc["t2"][:, nt:nt + 1], OP.mult, OP.mult),
            rd=[b_sc["t1"], b_sc["t2"]], wr=[b_sc["nm"]])
        prog.op("scalar", lambda e, b=b, nt=nt: e.activation(
            qx8s[:], xstC[b][:], AF.Identity, bias=sc["nm"][:, nt:nt + 1],
            scale=sc["t2"][:, nt:nt + 1]),
            rd=[b_xsC[b]], rdp=[b_sc["nm"], b_sc["t2"]], wr=[b_q8])
        for ct in range(16):
            prog.op("tensor", lambda e, ct=ct: e.transpose(
                pF8v[:, ct * 128:(ct + 1) * 128, 0:1],
                qx8s[:, ct * 128:(ct + 1) * 128],
                idq[:]), rd=[b_q8, b_idq], wr=[b_pF] if ct in (0, 15) else ())
        prog.op("scalar", lambda e, nt=nt: e.activation(
            qxT8[:].rearrange("p (ct n) -> p ct n", n=S)[:, :, nt * 128:(nt + 1) * 128],
            pF8g(16), AF.Copy),
            rd=[b_pF], wr=[b_qx])
    if "qxT" in dbg_t:
        prog.dma("sync", dbg_t["qxT"].ap(), qxT8[:, :], rd=[b_qx], wr=[NB()])

    def prep_ic(ic):
        bw = ic % 3
        chains = []
        for fam, (wf, b_wf, wsrc, wtb, b_wtb) in enumerate((
                (wgf, b_wgf, wg_t, wgT, b_wgT), (wuf, b_wuf, wu_t, wuT, b_wuT))):
            for halfq in range(4):
                ich = ic * 4 + halfq
                mi = fam * 4 + halfq
                b_m = b_wm1[mi]
                sp = halfq % 2
                ws_i = (fam * 4 + halfq) % 4
                def chain(sp=sp, wf=wf, b_wf=b_wf, mi=mi, b_m=b_m,
                          ws_i=ws_i, halfq=halfq, wsrc=wsrc, ich=ich):
                    prog.op("gpsimd", lambda e: e.indirect_dma_start(
                            wf[sp][:], None, wsrc.ap(),
                            bass.IndirectOffsetOnAxis(idxb[:, ich:ich + 1], 0)),
                        rd=[b_idx], wr=[b_wf[sp]],
                        sem=prog.buf_dsem(b_wf[sp]), inck=16)
                    prog.op("vector", lambda e: e.reduce_max(
                        wm1[:, 2 * mi:2 * mi + 1], wf[sp][:], AX.X,
                        apply_absolute_value=True),
                        rd=[b_wf[sp]], wr=[b_m])
                    prog.op("vector", lambda e: e.reciprocal(
                        wm1[:, 2 * mi + 1:2 * mi + 2], wm1[:, 2 * mi:2 * mi + 1]),
                        rd=[b_m], wr=[b_m])
                    prog.op("vector", lambda e: e.tensor_scalar_mul(
                        wm1[:, 2 * mi + 1:2 * mi + 2],
                        wm1[:, 2 * mi + 1:2 * mi + 2],
                        224.0), rd=[b_m], wr=[b_m])
                    prog.op("vector", lambda e: e.tensor_scalar_mul(
                        wm1[:, 2 * mi:2 * mi + 1], wm1[:, 2 * mi:2 * mi + 1],
                        1.0 / 224.0), rd=[b_m], wr=[b_m])
                    prog.op("scalar", lambda e: e.activation(
                        w8s[sp][:], wf[sp][:], AF.Identity,
                        scale=wm1[:, 2 * mi + 1:2 * mi + 2]),
                        rd=[b_wf[sp]], rdp=[b_m], wr=[b_w8[sp]])
                    prog.op("vector", lambda e: e.tensor_scalar_mul(
                        wsc2[ws_i][:], w8s[sp][:], wm1[:, 2 * mi:2 * mi + 1]),
                        rd=[b_w8[sp]], rdp=[b_m], wr=[b_ws2[ws_i]])
                def chain2(ws_i=ws_i, bw=bw, wtb=wtb, b_wtb=b_wtb,
                           halfq=halfq):
                    prog.trans(wtb[bw][:].rearrange("p (ct f) -> p ct f", f=512)
                               [:, :, halfq * 128:(halfq + 1) * 128],
                               wsc2[ws_i][:].rearrange("p (ct f) -> p ct f", f=128),
                               rd=[b_ws2[ws_i]], wr=[b_wtb[bw]])
                chains.append((chain, chain2))
        return chains

    def mm_ic(ic, bg):
        bw = ic % 3
        done = [0]
        flushed = [0]
        for nt in range(NT):
            want = len(bg) * (nt + 1) // NT
            while done[0] < want:
                bg[done[0]][0]()
                done[0] += 1
            if len(bg) and nt in (NT // 2 - 1, NT - 1):
                while flushed[0] < done[0]:
                    bg[flushed[0]][1]()
                    flushed[0] += 1
            pg, b_pg = pG[2 * (nt % 2)], b_pG[2 * (nt % 2)]
            pu, b_pu = pG[2 * (nt % 2) + 1], b_pG[2 * (nt % 2) + 1]
            for ct in range(16):
                prog.op("tensor", lambda e, pg=pg, ct=ct, nt=nt, bw=bw: e.matmul(
                    pg[:], qxT8[:, ct * S + nt * 128: ct * S + nt * 128 + 128],
                    wgT[bw][:, ct * 512:(ct + 1) * 512],
                    start=(ct == 0), stop=(ct == 15)),
                    rd=[b_qx, b_wgT[bw]], wr=[b_pg] if ct in (0, 15) else ())
                prog.op("tensor", lambda e, pu=pu, ct=ct, nt=nt, bw=bw: e.matmul(
                    pu[:], qxT8[:, ct * S + nt * 128: ct * S + nt * 128 + 128],
                    wuT[bw][:, ct * 512:(ct + 1) * 512],
                    start=(ct == 0), stop=(ct == 15)),
                    rd=[b_qx, b_wuT[bw]], wr=[b_pu] if ct in (0, 15) else ())
            prog.op("scalar", lambda e, pg=pg, nt=nt: e.activation(
                sgb[:], pg[:], AF.Silu, scale=qscb[:, nt:nt + 1]),
                rd=[b_pg], rdp=[b_qsc], wr=[b_sg])
            prog.op("vector", lambda e, pu=pu, nt=nt: e.scalar_tensor_tensor(
                tst[:], pu[:], qscb[:, nt:nt + 1], sgb[:], OP.mult, OP.mult),
                rd=[b_pu, b_sg], rdp=[b_qsc], wr=[b_tst])
            prog.dma("sync", tsc.ap()[nt * 128:(nt + 1) * 128,
                                      ic * 512:(ic + 1) * 512],
                     tst[:], rd=[b_tst], wr=[b_ts])

    ch0 = prep_ic(0)
    for c1, c2 in ch0:
        c1()
        c2()
    ch1 = prep_ic(1)
    for c1, c2 in ch1:
        c1()
        c2()
    for ic in range(11):
        bg = prep_ic(ic + 2) if ic + 2 < 11 else []
        mm_ic(ic, bg)
    if "tsc" in dbg_t:
        prog.dma("sync", dbg_t["tsc"].ap(), tsc.ap(), rd=[b_ts], wr=[NB()])

    prog.barrier(prog.dsems + list(prog.csem.values()))
    esC.close()
    esC2 = ExitStack()
    sbC2 = lambda n, sh, dt: esC2.enter_context(nc.sbuf_tensor(n, list(sh), dt))
    qtT8 = sbC2("qtT8", [128, 44 * S], FP8); b_qt = NB()
    esC2b = ExitStack()
    sbC2b = lambda n, sh, dt: esC2b.enter_context(nc.sbuf_tensor(n, list(sh), dt))
    tld = [sbC2b(f"tld{i}", [128, IFF], BF16) for i in range(2)]; b_tld = NB(2)
    qt8s = sbC2b("qt8s", [128, IFF], FP8); b_qt8 = NB()
    tdum = sbC2b("tdum", [128, IFF], BF16); b_tdum = NB()

    for nt in range(NT):
        b = nt % 2
        prog.dma("sync", tld[b][:], tsc.ap()[nt * 128:(nt + 1) * 128, :],
                 rd=[b_ts], wr=[b_tld[b]])
        prog.op("vector", lambda e, b=b, nt=nt: e.reduce_max(
            tmaxb[:, nt:nt + 1], tld[b][:], AX.X, apply_absolute_value=True),
            rd=[b_tld[b]], wr=[b_tmx])
        prog.op("vector", lambda e, nt=nt: e.reciprocal(
            sc["t1"][:, nt:nt + 1], tmaxb[:, nt:nt + 1]),
            rd=[b_tmx], wr=[b_sc["t1"]])
        prog.op("vector", lambda e, nt=nt: e.tensor_scalar_mul(
            sc["t1"][:, nt:nt + 1], sc["t1"][:, nt:nt + 1], 224.0),
            rd=[b_sc["t1"]], wr=[b_sc["t1"]])
        prog.op("vector", lambda e, nt=nt: e.tensor_scalar_mul(
            tscb[:, nt:nt + 1], tmaxb[:, nt:nt + 1], 1.0 / 224.0),
            rd=[b_tmx], wr=[b_tsc])
        prog.op("scalar", lambda e, b=b, nt=nt: e.activation(
            qt8s[:], tld[b][:], AF.Identity, scale=sc["t1"][:, nt:nt + 1]),
            rd=[b_tld[b]], rdp=[b_sc["t1"]], wr=[b_qt8])
        for hq in range(2):
            ntl = 22
            for it in range(ntl):
                itg = hq * 22 + it
                prog.op("tensor", lambda e, itg=itg, it=it: e.transpose(
                    pF8v[:, it * 128:(it + 1) * 128, 0:1],
                    qt8s[:, itg * 128:(itg + 1) * 128], idq[:]),
                    rd=[b_qt8, b_idq], wr=[b_pF] if it in (0, ntl - 1) else ())
            prog.op("scalar", lambda e, hq=hq, nt=nt, ntl=ntl: e.activation(
                qtT8[:].rearrange("p (it n) -> p it n", n=S)
                [:, hq * 22:hq * 22 + ntl, nt * 128:(nt + 1) * 128],
                pF8g(ntl), AF.Copy),
                rd=[b_pF], wr=[b_qt])
    if "qtT" in dbg_t:
        prog.dma("sync", dbg_t["qtT"].ap(), qtT8[:, :], rd=[b_qt], wr=[NB()])

    if "D" not in phases:
        prog.barrier(prog.dsems + list(prog.csem.values()))
        esC2b.close()
        esC2.close()
        esF8.close()
        es_root.close()
        prog.emit()
        return nc, dbg_t

    prog.barrier(prog.dsems + list(prog.csem.values()))
    esC2b.close()
    esD = ExitStack()
    sbD = lambda n, sh, dt: esD.enter_context(nc.sbuf_tensor(n, list(sh), dt))
    qwdT = [sbD(f"qwdT{i}", [128, 44 * 128], FP8) for i in range(2)]; b_qw = NB(2)
    wdf = [sbD(f"wdf{i}", [128, IFF], F32) for i in range(2)]; b_wdD = NB(2)
    wd8s = [sbD(f"wd8s{i}", [128, 512], FP8) for i in range(4)]; b_w8D = NB(4)
    wdr = [sbD(f"wdr{i}", [128, 1], F32) for i in range(2)]; b_wdr = NB(2)
    ytst = sbD("ytst", [128, S], BF16); b_yt = NB()
    ytT = sbD("ytT", [128, 16 * 128], BF16); b_yT = NB()
    pY = [esD.enter_context(nc.psum_tensor(f"pY{i}", [128, 512], F32))
          for i in range(2)]
    b_pY = NB(2)

    def prepD(ht):
        bh = ht % 2
        prog.dma("sync", wdf[bh][:], wd_t.ap()[ht * 128:(ht + 1) * 128, :],
                 wr=[b_wdD[bh]])
        prog.op("vector", lambda e, bh=bh, ht=ht: e.reduce_max(
            wdmax[:, ht:ht + 1], wdf[bh][:], AX.X, apply_absolute_value=True),
            rd=[b_wdD[bh]], wr=[b_wdm])
        prog.op("vector", lambda e, bh=bh, ht=ht: e.reciprocal(
            wdr[bh][:], wdmax[:, ht:ht + 1]), rd=[b_wdm], wr=[b_wdr[bh]])
        prog.op("vector", lambda e, ht=ht: e.tensor_scalar_mul(
            wdsc[:, ht:ht + 1], wdmax[:, ht:ht + 1], 1.0 / 224.0),
            rd=[b_wdm], wr=[b_wds])
        prog.op("vector", lambda e, bh=bh: e.tensor_scalar_mul(
            wdr[bh][:], wdr[bh][:], 224.0), rd=[b_wdr[bh]], wr=[b_wdr[bh]])
        chains = []
        for icc in range(11):
            bw = 2 * bh + icc % 2

            def chain(bh=bh, bw=bw, icc=icc):
                prog.op("vector", lambda e: e.tensor_scalar_mul(
                    wd8s[bw][:], wdf[bh][:, icc * 512:(icc + 1) * 512], wdr[bh][:]),
                    rd=[b_wdD[bh]], rdp=[b_wdr[bh]], wr=[b_w8D[bw]])
                for t4 in range(4):
                    prog.op("tensor", lambda e, t4=t4: e.transpose(
                        pF8v[:, t4 * 128:(t4 + 1) * 128, 0:1],
                        wd8s[bw][:, t4 * 128:(t4 + 1) * 128],
                        idq[:]), rd=[b_w8D[bw], b_idq],
                        wr=[b_pF] if t4 in (0, 3) else ())
                prog.op("vector", lambda e: e.tensor_copy(
                    qwdT[bh][:, icc * 512:(icc + 1) * 512].rearrange(
                        "p (t f) -> p t f", f=128),
                    pF8g(4)[:, :, :, 0]), rd=[b_pF], wr=[b_qw[bh]])
            chains.append(chain)
        return chains

    def mmD(ht, bg):
        bh = ht % 2
        nchunks = [(n0, min(512, S - n0)) for n0 in range(0, S, 512)]
        done = [0]
        for ncc, (n0, nw) in enumerate(nchunks):
            want = len(bg) * (ncc + 1) // len(nchunks)
            while done[0] < want:
                bg[done[0]]()
                done[0] += 1
            py, b_py = pY[ncc % 2], b_pY[ncc % 2]
            for ic44 in range(44):
                prog.op("tensor", lambda e, ic44=ic44, n0=n0, nw=nw, py=py, bh=bh:
                        e.matmul(
                    py[:, 0:nw], qwdT[bh][:, ic44 * 128:(ic44 + 1) * 128],
                    qtT8[:, ic44 * S + n0: ic44 * S + n0 + nw],
                    start=(ic44 == 0), stop=(ic44 == 43)),
                    rd=[b_qw[bh], b_qt], wr=[b_py] if ic44 in (0, 43) else ())
            prog.op("vector", lambda e, n0=n0, nw=nw, ht=ht, py=py: e.tensor_scalar_mul(
                ytst[:, n0:n0 + nw], py[:, 0:nw], wdsc[:, ht:ht + 1]),
                rd=[b_py], rdp=[b_wds], wr=[b_yt])
        prog.trans(ytT[:, 0:NT * 128].rearrange("p (t f) -> p t f", f=128),
                   ytst[:, 0:S].rearrange("p (t f) -> p t f", f=128),
                   rd=[b_yt], wr=[b_yT])
        prog.dma("sync", ysc.ap()[:, ht * 128:(ht + 1) * 128]
                 .rearrange("(t p) f -> p t f", p=128),
                 ytT[:, 0:NT * 128].rearrange("p (t f) -> p t f", f=128),
                 rd=[b_yT], wr=[b_ys])

    chD = prepD(0)
    for c in chD:
        c()
    for ht in range(16):
        bgD = prepD(ht + 1) if ht + 1 < 16 else []
        mmD(ht, bgD)

    yld = [sbD("yld0", [128, E], BF16)] * 2; b_yl0 = NB()
    b_yl = [b_yl0, b_yl0]
    xmld = [sbD("xmld0", [128, E], F32)] * 2; b_xl0 = NB()
    b_xl = [b_xl0, b_xl0]
    outst = [sbD("outst0", [128, E], F32)] * 2; b_os0 = NB()
    b_os = [b_os0, b_os0]
    for nt in range(NT):
        b = nt % 2
        prog.dma("sync", yld[b][:], ysc.ap()[nt * 128:(nt + 1) * 128, :],
                 rd=[b_ys], wr=[b_yl[b]])
        prog.dma("sync", xmld[b][:], xm.ap()[nt * 128:(nt + 1) * 128, :],
                 rd=[b_xm], wr=[b_xl[b]])
        prog.op("vector", lambda e, b=b, nt=nt: e.scalar_tensor_tensor(
            outst[b][:], yld[b][:], tscb[:, nt:nt + 1], xmld[b][:],
            OP.mult, OP.add), rd=[b_yl[b], b_xl[b]], rdp=[b_tsc], wr=[b_os[b]])
        prog.dma("sync", out_t.ap()[nt * 128:(nt + 1) * 128, :], outst[b][:], rd=[b_os[b]], wr=[b_out])

    prog.barrier(prog.dsems + list(prog.csem.values()))
    esD.close()
    esC2.close()
    esF8.close()
    es_root.close()
    prog.emit()
    return nc, dbg_t



def _reorder_cols(w):
    J = w.shape[1]
    idx = np.arange(128).reshape(64, 2).T.reshape(-1)
    cols = np.concatenate([b * 128 + idx for b in range(J // 128)])
    return np.ascontiguousarray(w[:, cols])


def _tables(S):
    pos = np.arange(S, dtype=np.float64)
    i = np.arange(128, dtype=np.float64)
    theta = 1.0 / 10000.0 ** (2 * i / 256.0)
    ang = pos[:, None] * theta[None, :]
    cos = np.cos(ang).astype(np.float32)
    sin = np.sin(ang).astype(np.float32)
    cosd = np.concatenate([cos[:, 0:64], cos[:, 64:128]] * 2, axis=1)
    sind = np.concatenate([sin[:, 0:64], sin[:, 64:128]] * 2, axis=1)
    return (np.ascontiguousarray(cosd, dtype=np.float32),
            np.ascontiguousarray(sind, dtype=np.float32))


def make_inputs(core_x, weights, S):
    cosd, sind = _tables(S)
    tri = np.tril(np.ones((128, 128), np.float32)).astype(ml_dtypes.bfloat16)
    m = {
        "x": np.ascontiguousarray(core_x),
        "wq": weights["wq_r"], "wk": weights["wk_r"], "wv": weights["wv"],
        "wo": weights["wo"], "wg": weights["wg"], "wu": weights["wu"],
        "wd": weights["wd"], "idx": weights["idx"],
        "lq1": weights["lq1"], "lq2": weights["lq2"],
        "lk1": weights["lk1"], "lk2": weights["lk2"],
        "rmsw": weights["rmsw"],
        "costab": cosd, "sintab": sind, "tri": tri,
        "identq": np.eye(128, dtype=np.float32).astype(ml_dtypes.float8_e4m3),
        "identf": np.eye(128, dtype=np.float32),
    }
    return m


def prep_weights(L0_attn_Wq, L0_attn_Wk, L0_attn_Wv, L0_attn_Wo,
                 L0_attn_lq1, L0_attn_lq2, L0_attn_lk1, L0_attn_lk2,
                 L0_attn_rms_w, L0_ffn_w_gate, L0_ffn_w_up, L0_ffn_w_down,
                 L0_ffn_down_reorder_index):
    f = np.asarray
    return {
        "wq_r": _reorder_cols(f(L0_attn_Wq, np.float32)),
        "wk_r": _reorder_cols(f(L0_attn_Wk, np.float32)),
        "wv": np.ascontiguousarray(f(L0_attn_Wv, np.float32)),
        "wo": np.ascontiguousarray(f(L0_attn_Wo, np.float32)),
        "wg": np.ascontiguousarray(f(L0_ffn_w_gate, np.float32)),
        "wu": np.ascontiguousarray(f(L0_ffn_w_up, np.float32)),
        "wd": np.ascontiguousarray(f(L0_ffn_w_down, np.float32)),
        "idx": np.ascontiguousarray(
            f(L0_ffn_down_reorder_index).astype(np.int32).reshape(IFF, 1)),
        "lq1": np.ascontiguousarray(f(L0_attn_lq1, np.float32)),
        "lq2": np.ascontiguousarray(f(L0_attn_lq2, np.float32)),
        "lk1": np.ascontiguousarray(f(L0_attn_lk1, np.float32)),
        "lk2": np.ascontiguousarray(f(L0_attn_lk2, np.float32)),
        "rmsw": np.ascontiguousarray(f(L0_attn_rms_w, np.float32).reshape(D, 1)),
    }


def kernel(L0_attn_x, L0_attn_Wq, L0_attn_Wk, L0_attn_Wv, L0_attn_Wo,
           L0_attn_lq1, L0_attn_lq2, L0_attn_lk1, L0_attn_lk2, L0_attn_rms_w,
           L0_ffn_w_gate, L0_ffn_w_up, L0_ffn_w_down, L0_ffn_up_reorder_index,
           L0_ffn_down_reorder_index):
    x = np.asarray(L0_attn_x, dtype=np.float32)
    BATCH, S, _ = x.shape
    weights = prep_weights(L0_attn_Wq, L0_attn_Wk, L0_attn_Wv, L0_attn_Wo,
                           L0_attn_lq1, L0_attn_lq2, L0_attn_lk1, L0_attn_lk2,
                           L0_attn_rms_w, L0_ffn_w_gate, L0_ffn_w_up,
                           L0_ffn_w_down, L0_ffn_down_reorder_index)
    nc, _ = build_nc(S=S)
    in_maps = [make_inputs(x[i], weights, S) for i in range(BATCH)]
    res = run_bass_kernel_spmd(nc, in_maps, core_ids=list(range(BATCH)))
    out = np.stack([np.asarray(r["out"], dtype=np.float32)
                    for r in res.results], axis=0)
    return out
